# version 03
# baseline (speedup 1.0000x reference)
import numpy as np
import ml_dtypes
import concourse.bass as bass
import concourse.mybir as mybir
from concourse.bass_utils import run_bass_kernel_spmd

F32 = mybir.dt.float32
BF16 = mybir.dt.bfloat16
AF = mybir.ActivationFunctionType
ALU = mybir.AluOpType


ENGS = ("pe", "dve", "act", "pool", "sp")
QUEUES = {"qsp": "sp", "qpool": "pool", "qact": "act"}
NLANES = 8
LANE_SPACES = tuple(f"{q}#{i}" for q in QUEUES for i in range(NLANES))
SPACES = ENGS + LANE_SPACES
SPACE_IDX = {s: i for i, s in enumerate(SPACES)}
NSPACE = len(SPACES)


class Buf:
    __slots__ = ("name", "wtoks", "readers", "excl")

    def __init__(self, name, excl=False):
        self.name = name
        self.wtoks = {}
        self.readers = []
        self.excl = excl

    def __repr__(self):
        return f"Buf({self.name})"


class Op:
    __slots__ = ("fn", "waits", "inc", "drain")

    def __init__(self, fn, waits):
        self.fn = fn
        self.waits = waits
        self.inc = False
        self.drain = False


import os as _os_mod
SELF_EDGES = _os_mod.environ.get("KK_SELF", "1") == "1"


class Sched:
    def __init__(self):
        self.ops = {e: [] for e in ENGS}
        self.dmacnt = {q: 0 for q in QUEUES}
        self.clk = {e: [-1] * NSPACE for e in ENGS}
        self.guar = {s: [] for s in SPACES}
        self.need_inc = {e: set() for e in ENGS}
        self.waited_self = {e: -1 for e in ENGS}
        self.out_dma_toks = []

    def buf(self, name, excl=False):
        return Buf(name, excl)

    def _collect(self, reads, writes, eng=None):
        need = []
        for b in reads:
            if b is None:
                continue
            need.extend(b.wtoks.items())
            if b.excl:
                need.extend(t for t in b.readers if t[0] != eng)
        for b in writes:
            if b is None:
                continue
            need.extend(b.wtoks.items())
            need.extend(b.readers)
        return need

    def _commit(self, tok, reads, writes):
        for b in reads:
            if b is not None:
                b.readers.append(tok)
        for b in writes:
            if b is not None:
                b.wtoks = {tok[0]: tok[1]}
                b.readers = []

    def op(self, eng, fn, reads=(), writes=(), drain=False):
        idx = len(self.ops[eng])
        myclk = self.clk[eng]
        waits = []
        for (s, i) in self._collect(reads, writes, eng):
            if s == eng and eng == "pe":
                continue
            if s == eng:
                if not SELF_EDGES or i <= self.waited_self[eng]:
                    continue
                waits.append((s, i))
                self.waited_self[eng] = i
                self.need_inc[s].add(i)
                continue
            if myclk[SPACE_IDX[s]] >= i:
                continue
            waits.append((s, i))
            g = self.guar[s][i]
            for k in range(NSPACE):
                if g[k] > myclk[k]:
                    myclk[k] = g[k]
            if s in ENGS:
                self.need_inc[s].add(i)
        myclk[SPACE_IDX[eng]] = idx
        self.guar[eng].append(tuple(myclk))
        o = Op(fn, waits)
        o.drain = drain
        self.ops[eng].append(o)
        self._commit((eng, idx), reads, writes)
        return (eng, idx)

    def dma(self, queue, fn, reads=(), writes=(), is_output=False):
        eng = QUEUES[queue]
        idx = len(self.ops[eng])
        d = self.dmacnt[queue]
        self.dmacnt[queue] = d + 1
        lane_space = f"{queue}#{d % NLANES}"
        use = d // NLANES
        myclk = self.clk[eng]
        waits = []
        need0 = [(lane_space, use - 1)] if use >= 1 else []
        for (s, i) in need0 + self._collect(reads, writes, eng):
            if s == eng and eng == "pe":
                continue
            if s == eng:
                if not SELF_EDGES or i <= self.waited_self[eng]:
                    continue
                waits.append((s, i))
                self.waited_self[eng] = i
                self.need_inc[s].add(i)
                continue
            if myclk[SPACE_IDX[s]] >= i:
                continue
            waits.append((s, i))
            g = self.guar[s][i]
            for k in range(NSPACE):
                if g[k] > myclk[k]:
                    myclk[k] = g[k]
            if s in ENGS:
                self.need_inc[s].add(i)
        myclk[SPACE_IDX[eng]] = idx
        self.guar[eng].append(tuple(myclk))
        qclk = list(myclk)
        qclk[SPACE_IDX[lane_space]] = use
        self.guar[lane_space].append(tuple(qclk))
        op = Op(fn, waits)
        op.inc = ("dma", queue, d)
        self.ops[eng].append(op)
        self._commit((lane_space, use), reads, writes)
        if is_output:
            self.out_dma_toks.append((lane_space, use))
        return (lane_space, use)

    def pe_fence(self):
        idx = len(self.ops["pe"]) - 1
        if idx < 0:
            return
        self.need_inc["pe"].add(idx)
        o = Op(None, [("pe", idx)])
        self.ops["pe"].append(o)
        myclk = self.clk["pe"]
        myclk[SPACE_IDX["pe"]] = idx + 1
        self.guar["pe"].append(tuple(myclk))

    def barrier_wait(self, eng, toks):
        fake = Buf("_barrier")
        fake.wtoks = {}
        waits = []
        myclk = self.clk[eng]
        for (s, i) in toks:
            if s == eng:
                continue
            if myclk[SPACE_IDX[s]] >= i:
                continue
            waits.append((s, i))
            g = self.guar[s][i]
            for k in range(NSPACE):
                if g[k] > myclk[k]:
                    myclk[k] = g[k]
            if s in ENGS:
                self.need_inc[s].add(i)
        idx = len(self.ops[eng])
        myclk[SPACE_IDX[eng]] = idx
        self.guar[eng].append(tuple(myclk))
        self.ops[eng].append(Op(None, waits))

    def finalize(self, nc, block, handles):
        rank = {}
        for e in ENGS:
            marks = sorted(self.need_inc[e])
            rmap = {}
            r = 0
            for m in marks:
                r += 1
                rmap[m] = r
            rank[e] = rmap
        sems = {}
        import contextlib
        self._stack = contextlib.ExitStack()
        for s in ENGS:
            sems[s] = self._stack.enter_context(nc.semaphore(f"sem_{s}"))
        lanes = {}
        for q in QUEUES:
            lanes[q] = [self._stack.enter_context(nc.semaphore(f"sem_{q}{i}"))
                        for i in range(NLANES)]
        self.sems = sems
        self.lanes = lanes

        def run_engine(e, ehandle, lo, hi):
            for i in range(lo, hi):
                op = self.ops[e][i]
                if op.drain:
                    ehandle.drain()
                for (s, j) in op.waits:
                    if s in ENGS:
                        ehandle.wait_ge(sems[s], rank[s][j])
                    else:
                        q, lane = s.split("#")
                        ehandle.wait_ge(lanes[q][int(lane)], 16 * (j + 1))
                if op.fn is None:
                    continue
                inst = op.fn()
                if isinstance(op.inc, tuple) and op.inc[0] == "dma":
                    q, dd = op.inc[1], op.inc[2]
                    inst.then_inc(lanes[q][dd % NLANES], 16)
                    if i in self.need_inc[e]:
                        inst.then_inc(sems[e], 1)
                elif i in self.need_inc[e]:
                    inst.then_inc(sems[e], 1)

        section = {
            "pe": block.tensor, "dve": block.vector, "act": block.scalar,
            "pool": block.gpsimd, "sp": block.sync,
        }
        import os
        if os.environ.get("KK_LIST"):
            e = os.environ["KK_LIST"]
            from collections import Counter
            cnt = Counter(op.fn.__code__.co_firstlineno for op in self.ops[e] if op.fn)
            print(f"[{e} ops by lambda lineno]", dict(sorted(cnt.items())))
        chunk = int(os.environ.get("BASS_BB_CHUNK", "400"))
        pos = {e: 0 for e in ENGS}
        done = False
        while not done:
            done = True
            for e in ENGS:
                n = len(self.ops[e])
                if pos[e] < n:
                    lo, hi = pos[e], min(pos[e] + chunk, n)
                    section[e](lambda eh, e=e, lo=lo, hi=hi: run_engine(e, eh, lo, hi))
                    pos[e] = hi
                    if pos[e] < n:
                        done = False

    def close(self):
        self._stack.close()

    def stats(self):
        return {e: (len(self.ops[e]), len(self.need_inc[e])) for e in ENGS} | {
            q: self.dmacnt[q] for q in QUEUES}



B, N, DM, H, DH = 16, 2048, 1024, 16, 64
NCORES = 8
SEQ_PER_CORE = B // NCORES
NT = N // 128
NGROUP = 8
DHID = 4096
EPS_LN = 1e-5
EPS_LA = 1e-6

bf16 = ml_dtypes.bfloat16



def build_program(debug_names=()):
    from contextlib import ExitStack
    nc = bass.Bass("TRN2", target_bir_lowering=False)
    S = Sched()
    st = ExitStack()
    dumps = {}

    def sb(name, shape, dt):
        return st.enter_context(nc.sbuf_tensor(name, shape, dt))

    def psum(name, shape, dt=F32):
        return st.enter_context(nc.psum_tensor(name, shape, dt))

    din = lambda name, shape, dt=F32: nc.dram_tensor(name, shape, dt, kind="ExternalInput").ap()
    dout = lambda name, shape, dt=F32: nc.dram_tensor(name, shape, dt, kind="ExternalOutput").ap()
    dint = lambda name, shape, dt: nc.dram_tensor(name, shape, dt, kind="Internal").ap()

    x_hbm = din("x", [SEQ_PER_CORE, N, DM])
    y_hbm = dout("y", [SEQ_PER_CORE, N, DM])
    w_qkl = din("w_qkl", [NGROUP, 128, 8, 256])
    w_kvv = din("w_kvv", [NGROUP, 128, 8, 384])
    w_g = din("w_g", [128, 8, 16, 128])
    w_go = din("w_go", [128, 8])
    w_o = din("w_o", [128, 8, 1024])
    w_haar = din("w_haar", [128, 8, 64])
    c_haarsc = din("c_haarsc", [128, 4])
    c_bg = din("c_bg", [128, 8])
    c_bgo = din("c_bgo", [128, 1])
    c_bo = din("c_bo", [128, 1024])
    c_s3 = din("c_s3", [128, 1024])
    w_1 = din("w_1", [32, 128, 8, 128])
    w_2 = din("w_2", [32, 128, 8, 128])
    w_h1 = din("w_h1", [32, 128, 32, 128])
    w_h2 = din("w_h2", [32, 128, 32, 128])
    w_3 = din("w_3", [128, 32, 1024])
    c_s1 = din("c_s1", [128, 32])
    c_s2 = din("c_s2", [128, 32])
    c_s1h = din("c_s1h", [128, 32])
    c_s2h = din("c_s2h", [128, 32])
    c_ident = din("c_ident", [128, 128], BF16)
    c_mlin = din("c_mlin", [128, 256], BF16)
    c_mloc0 = din("c_mloc0", [128, 256], BF16)
    c_mlocn = din("c_mlocn", [128, 256], BF16)
    c_lbt = din("c_lbt", [128, 4, 128], BF16)
    c_lenr = din("c_lenr", [128, 4])
    c_ones = din("c_ones", [128, 128], BF16)

    qklb = dint("qklb", [NGROUP, 128, 8, 256], BF16)
    kvvb = dint("kvvb", [NGROUP, 128, 8, 384], BF16)
    wgb_d = dint("wgb_d", [128, 8, 16, 128], BF16)
    wob_d = dint("wob_d", [128, 8, 1024], BF16)
    w1b = dint("w1b", [32, 128, 8, 128], BF16)
    w2b = dint("w2b", [32, 128, 8, 128], BF16)
    h1b = dint("h1b", [32, 128, 32, 128], BF16)
    h2b = dint("h2b", [32, 128, 32, 128], BF16)
    w3b = dint("w3b", [128, 32, 1024], BF16)
    y1d = dint("y1d", [SEQ_PER_CORE, NT, 128, 1024], F32)

    H_ar = sb("H_ar", [128, 32768], BF16)
    G_ar = sb("G_ar", [128, 16384], BF16)
    WB = sb("WB", [128, 8192], BF16)
    SCR = sb("SCR", [128, 4096], F32)
    qkl_bf = sb("qkl_bf", [128, 2048], BF16)
    kvv_bf = sb("kvv_bf", [128, 3072], BF16)
    qf = sb("qf", [128, 2048], BF16)
    klf = sb("klf", [128, 2048], BF16)
    ktm = sb("ktm", [128, 2048], BF16)
    fktm = sb("fktm", [128, 2048], BF16)
    fkf = sb("fkf", [128, 2048], BF16)
    vtm = sb("vtm", [128, 2048], BF16)
    vltm = sb("vltm", [128, 2048], BF16)
    vlvl = sb("vlvl", [128, 2048], BF16)
    bmvf = sb("bmvf", [128, 2048], BF16)
    atil = sb("atil", [128, 768], BF16)
    snap = sb("snap", [128, 975], BF16)
    ptil = sb("ptil", [128, 1024], BF16)
    fscr = sb("fscr", [128, 512], BF16)
    acc = sb("acc", [128, 2048], F32)
    tmp1 = sb("tmp1", [128, 1024], BF16)
    dtile = sb("dtile", [128, 1024], BF16)
    dfm = sb("dfm", [128, 1024], BF16)
    gatet = sb("gatet", [128, 1024], BF16)
    alph = sb("alph", [128, 32], F32)
    abuf = sb("abuf", [128, 512], BF16)
    bbuf = sb("bbuf", [128, 512], BF16)
    stat = sb("stat", [128, 48], F32)
    sw_sb = sb("sw_sb", [128, 16], F32)
    ident = sb("ident", [128, 128], BF16)
    mlin = sb("mlin", [128, 256], BF16)
    mloc0 = sb("mloc0", [128, 256], BF16)
    mlocn = sb("mlocn", [128, 256], BF16)
    lbt = sb("lbt", [128, 512], BF16)
    lenr = sb("lenr", [128, 4], F32)
    onesb = sb("onesb", [128, 128], BF16)
    haarw = sb("haarw", [128, 512], BF16)
    haarsc = sb("haarsc", [128, 4], F32)
    bg_sb = sb("bg_sb", [128, 8], F32)
    bgo_sb = sb("bgo_sb", [128, 1], F32)
    s1_sb = sb("s1_sb", [128, 32], F32)
    s2_sb = sb("s2_sb", [128, 32], F32)
    s1h_sb = sb("s1h_sb", [128, 32], F32)
    s2h_sb = sb("s2h_sb", [128, 32], F32)
    wgo_sb = sb("wgo_sb", [128, 8], BF16)
    cpool = sb("cpool", [128, 2], F32)
    cpool2 = sb("cpool2", [128, 1], F32)

    ps_state = psum("ps_state", [64, 130])
    ps_A = psum("ps_A", [128, 512])
    ps_tr = psum("ps_tr", [128, 1024], BF16)
    trB = ps_tr[:, :].bitcast(F32)
    ps_q = [psum(f"ps_q{i}", [128, 512]) for i in range(4)]
    ps_nrm = psum("ps_nrm", [128, 64])

    B = S.buf
    b_consts = B("consts")
    b_Hs = [B(f"H{t}") for t in range(NT)]
    b_G = [B(f"G{t}") for t in range(NT)]
    b_WB = B("WB")
    b_SCR = B("SCR")
    b_qkl = B("qkl_bf"); b_kvv = B("kvv_bf")
    b_qf = B("qf"); b_klf = B("klf"); b_ktm = B("ktm"); b_fktm = B("fktm")
    b_fkf = B("fkf"); b_vtm = B("vtm"); b_vltm = B("vltm"); b_vlvl = B("vlvl")
    b_bmvf = B("bmvf"); b_atil = B("atil"); b_snap = B("snap"); b_ptil = B("ptil")
    b_fscr = B("fscr"); b_acc = B("acc"); b_tmp1 = B("tmp1")
    b_atil4 = [B(f"atil{i}") for i in range(4)]
    b_psA2 = [B("psA0", excl=True), B("psA1", excl=True)]
    b_ptil2 = [B("pt0"), B("pt1")]
    b_gatet = B("gatet"); b_alph = B("alph")
    b_abuf = B("abuf"); b_bbuf = B("bbuf"); b_stat = B("stat"); b_sw = B("sw_sb")
    b_ps_state = B("ps_state", excl=True); b_ps_A = B("ps_A", excl=True)
    b_ps_tr = B("ps_tr", excl=True)
    b_trB2 = [B("trB0", excl=True), B("trB1", excl=True)]
    b_ps_nrm = B("ps_nrm", excl=True)
    b_y1d = B("y1d")
    b_wcache = {n: B(n) for n in
                ["qklb", "kvvb", "wgb_d", "wob_d", "w1b", "w2b", "h1b", "h2b", "w3b"]}

    def dump(name, ap, reads):
        if name not in debug_names or name in dumps:
            return
        shape = list(ap.shape)
        d = nc.dram_tensor(f"dbg_{name}", shape, ap.dtype, kind="ExternalOutput").ap()
        S.dma("qpool", lambda d=d, ap=ap: nc.gpsimd.dma_start(out=d, in_=ap),
              reads=reads, is_output=True)
        dumps[name] = shape

    def mm(out, lhsT, rhs, start, stop, reads, writes, skip=False):
        S.op("pe", lambda o=out, l=lhsT, r=rhs, a=start, b=stop, sk=skip:
             nc.tensor.matmul(out=o, lhsT=l, rhs=r, start=a, stop=b,
                              skip_group_check=sk),
             reads=reads, writes=writes)

    def tr(out, in_, reads, writes):
        S.op("pe", lambda o=out, i=in_: nc.tensor.transpose(out=o, in_=i, identity=ident[:, :]),
             reads=list(reads) + [b_consts], writes=writes)

    def dve(meth, reads, writes, **kw):
        S.op("dve", lambda meth=meth, kw=kw: getattr(nc.vector, meth)(**kw),
             reads=reads, writes=writes)

    def act(meth, reads, writes, **kw):
        S.op("act", lambda meth=meth, kw=kw: getattr(nc.scalar, meth)(**kw),
             reads=reads, writes=writes)

    def pool(meth, reads, writes, **kw):
        S.op("pool", lambda meth=meth, kw=kw: getattr(nc.gpsimd, meth)(**kw),
             reads=reads, writes=writes)

    pool("memset", [], [b_consts], ap=cpool[:, 0:1], constant=0.0)
    pool("memset", [], [b_consts], ap=cpool[:, 1:2], constant=float(EPS_LN))
    nc.const_aps.register(F32, 0.0, cpool[:, 0:1])
    nc.const_aps.register(F32, float(EPS_LN), cpool[:, 1:2])
    pool("memset", [], [b_consts], ap=cpool2[:, 0:1], constant=-1.0)
    nc.const_aps.register(F32, -1.0, cpool2[:, 0:1])

    for dst, src in [
        (ident[:, :], c_ident), (mlin[:, :], c_mlin), (mloc0[:, :], c_mloc0),
        (mlocn[:, :], c_mlocn), (lbt[:, :], c_lbt.rearrange("p a b -> p (a b)")),
        (lenr[:, :], c_lenr), (onesb[:, :], c_ones),
        (haarsc[:, :], c_haarsc), (bgo_sb[:, :], c_bgo),
        (s1_sb[:, :], c_s1), (s2_sb[:, :], c_s2),
        (s1h_sb[:, :], c_s1h), (s2h_sb[:, :], c_s2h),
    ]:
        S.dma("qsp", lambda d=dst, s=src: nc.sync.dma_start(out=d, in_=s),
              writes=[b_consts])
    S.dma("qpool", lambda: nc.gpsimd.dma_start(
        out=haarw[:, :], in_=w_haar.rearrange("p a b -> p (a b)")), writes=[b_consts])
    S.dma("qsp", lambda: nc.sync.dma_start(out=bg_sb[:, :], in_=c_bg), writes=[b_consts])
    S.dma("qsp", lambda: nc.sync.dma_start(out=SCR[:, 2048:2056], in_=w_go), writes=[b_SCR])
    dve("tensor_copy", [b_SCR], [b_consts], out=wgo_sb[:, :], in_=SCR[:, 2048:2056])
    act("activation", [b_consts], [b_sw], out=sw_sb[:, 8:12], in_=haarsc[:, :],
        func=AF.Exp, bias=0.0, scale=1.0, accum_out=sw_sb[:, 5:6])
    act("activation", [b_sw], [b_sw], out=sw_sb[:, 6:7], in_=sw_sb[:, 5:6],
        func=AF.Ln, bias=0.0, scale=1.0)
    act("activation", [b_sw], [b_sw], out=sw_sb[:, 7:8], in_=sw_sb[:, 6:7],
        func=AF.Exp, bias=0.0, scale=-1.0)
    S.op("dve", lambda: nc.vector.tensor_scalar(
        out=sw_sb[:, 0:4], in0=sw_sb[:, 8:12], scalar1=sw_sb[:, 7:8],
        scalar2=None, op0=ALU.mult), reads=[b_sw], writes=[b_sw])
    act("activation", [b_sw], [b_sw], out=sw_sb[:, 12:16], in_=sw_sb[:, 0:4],
        func=AF.Ln, bias=0.0, scale=1.0)
    pool("memset", [], [b_ptil2[0], b_ptil2[1]], ap=ptil[:, :], constant=0.0)

    pc_jobs = []

    def pc_add(src4, dst4, items, bname):
        for i in range(items):
            pc_jobs.append((src4[i, :, :, :].rearrange("p a b -> p (a b)"),
                            dst4[i, :, :, :].rearrange("p a b -> p (a b)"), bname))

    pc_add(w_qkl, qklb, NGROUP, "qklb")
    pc_add(w_kvv, kvvb, NGROUP, "kvvb")
    pc_jobs.append((w_g.rearrange("p a b c -> p (a b c)"),
                    wgb_d.rearrange("p a b c -> p (a b c)"), "wgb_d"))
    pc_jobs.append((w_o.rearrange("p a b -> p (a b)"),
                    wob_d.rearrange("p a b -> p (a b)"), "wob_d"))
    for hhalf in range(4):
        pc_jobs.append((w_3[:, hhalf * 8:(hhalf + 1) * 8, :].rearrange("p a b -> p (a b)"),
                        w3b[:, hhalf * 8:(hhalf + 1) * 8, :].rearrange("p a b -> p (a b)"),
                        "w3b"))
    pc_add(w_1, w1b, 32, "w1b")
    pc_add(w_2, w2b, 32, "w2b")
    pc_add(w_h1, h1b, 32, "h1b")
    pc_add(w_h2, h2b, 32, "h2b")

    pc_state = {"i": 0}

    def plan_precast(frac):
        n = len(pc_jobs)
        hi = min(n, pc_state["i"] + max(1, int(n * frac + 1)))
        for k in range(pc_state["i"], hi):
            (sa, da, bn) = pc_jobs[k]
            S.dma("qpool", lambda d=da, s=sa: nc.gpsimd.dma_start(out=d, in_=s),
                  reads=[], writes=[b_wcache[bn]])
        pc_state["i"] = hi

    def rebind(newbufs, oldbufs):
        merged = {}
        for ob in oldbufs:
            for sp_, i in ob.wtoks.items():
                merged[sp_] = max(merged.get(sp_, -1), i)
            for (sp_, i) in ob.readers:
                merged[sp_] = max(merged.get(sp_, -1), i)
        for nb in newbufs:
            nb.wtoks = dict(merged)
            nb.readers = []

    b_lstrip = [B(f"ls{i}") for i in range(16)]
    b_gstrip = [B(f"gs{i}") for i in range(16)]
    b_psq = [B(f"psq{i}", excl=True) for i in range(4)]
    b_xt = [B("xt0"), B("xt1")]
    b_trash = B("trash")
    b_ao = B("ao")
    b_lt = [B("lt0"), B("lt1")]
    b_wb = [B("wb")]
    b_h = [B(f"hf{i}") for i in range(32)]
    b_g1 = [B(f"g1f{i}") for i in range(32)]
    b_y1s = [B("y1s0"), B("y1s1")]
    b_fout = B("fout")
    b_alp = b_alph

    AX = mybir.AxisListType.X

    def ts_op(out, in0, s1, s2, op0, op1, reads, writes):
        if s2 is None:
            S.op("dve", lambda o=out, i=in0, a=s1, p0=op0:
                 nc.vector.tensor_scalar(out=o, in0=i, scalar1=a, scalar2=None, op0=p0),
                 reads=reads, writes=writes)
        else:
            S.op("dve", lambda o=out, i=in0, a=s1, b2=s2, p0=op0, p1=op1:
                 nc.vector.tensor_scalar(out=o, in0=i, scalar1=a, scalar2=b2, op0=p0, op1=p1),
                 reads=reads, writes=writes)

    def ln_chain(xt_ap, b_x_ap, out_ap, b_out, scol, extra_reads=()):
        trash = SCR[:, 3072:4096]
        act("activation", [b_x_ap] + list(extra_reads), [b_trash, b_stat],
            out=trash, in_=xt_ap, func=AF.Square, accum_out=stat[:, scol + 1:scol + 2])
        act("activation", [b_x_ap], [b_trash, b_stat],
            out=trash, in_=xt_ap, func=AF.Copy, accum_out=stat[:, scol + 0:scol + 1])
        act("activation", [b_stat], [b_stat], out=stat[:, scol + 2:scol + 3],
            in_=stat[:, scol:scol + 1], func=AF.Copy, bias=0.0, scale=1.0 / 1024)
        act("activation", [b_stat], [b_stat], out=stat[:, scol + 3:scol + 4],
            in_=stat[:, scol + 2:scol + 3], func=AF.Square, bias=0.0, scale=1.0)
        S.op("dve", lambda o=stat[:, scol + 4:scol + 5], i=stat[:, scol + 1:scol + 2],
             m2=stat[:, scol + 3:scol + 4]:
             nc.vector.scalar_tensor_tensor(out=o, in0=i, scalar=1.0 / 1024, in1=m2,
                                            op0=ALU.mult, op1=ALU.subtract),
             reads=[b_stat], writes=[b_stat])
        act("activation", [b_stat], [b_stat], out=stat[:, scol + 5:scol + 6],
            in_=stat[:, scol + 4:scol + 5], func=AF.Ln, bias=float(EPS_LN), scale=1.0)
        act("activation", [b_stat], [b_stat], out=stat[:, scol + 6:scol + 7],
            in_=stat[:, scol + 5:scol + 6], func=AF.Exp, bias=0.0, scale=-0.5)
        ts_op(out_ap, xt_ap, stat[:, scol + 2:scol + 3], stat[:, scol + 6:scol + 7],
              ALU.subtract, ALU.mult, [b_x_ap, b_stat], [b_out])

    def phi3(ps_or_sbuf_ap, b_in, out_ap, b_out, fs_off, width, fs_ap=None):
        fsl = fscr[:, fs_off:fs_off + width] if fs_ap is None else fs_ap
        ts_op(fsl, ps_or_sbuf_ap, 0.0, None, ALU.min, None, [b_in], [b_fscr])
        act("activation", [b_fscr], [b_fscr], out=fsl, in_=fsl, func=AF.Exp,
            bias=0.0, scale=1.0)
        ts_op(out_ap, ps_or_sbuf_ap, 0.0, None, ALU.max, None, [b_in], [b_out])
        dve("tensor_tensor", [b_out, b_fscr], [b_out], out=out_ap, in0=out_ap,
            in1=fsl, op=ALU.add)

    def G_ap(t):
        return G_ar[:, :].rearrange("p (a b) -> p a b", a=8)[:, :, t * 128:(t + 1) * 128]

    def strip_ap(glob, t):
        off = 16384 if glob else 0
        return H_ar[:, off:off + 16384].rearrange("p (hh q) -> p hh q", hh=16)[
            :, :, 64 * t:64 * t + 64]

    def bc16(ap_col16):
        a = ap_col16.rearrange("p (a b) -> p a b", b=1)
        return bass.AP(a.tensor, a.offset, [list(x) for x in a.ap[:-1]] + [[0, 64]])

    def phase_x(s):
        rebind(b_G, b_g1 + b_G)
        rebind(b_xt + [b_trash, b_ao], b_xt + [b_trash, b_ao, b_SCR] + b_y1s + [b_fout])
        if _os.environ.get("KK_NOX", "0") == "1":
            pool("memset", [], b_G, ap=G_ar[:, 0:1024], constant=0.01)
            return
        for t in range(NT):
            sl = t % 2
            xt = SCR[:, sl * 1024:(sl + 1) * 1024]
            S.dma("qsp", lambda d=xt, s_=s, t=t: nc.sync.dma_start(
                out=d, in_=x_hbm[s_, t * 128:(t + 1) * 128, :]), writes=[b_xt[sl]])
            lt = (dtile, dfm)[sl]
            ln_chain(xt, b_xt[sl], lt[:, :], b_lt[sl], (t % 2) * 8)
            for fc in range(8):
                tr(ps_tr[:, fc * 128:(fc + 1) * 128], lt[:, fc * 128:(fc + 1) * 128],
                   [b_lt[sl]], [b_ps_tr] if fc in (0, 7) else [])
            act("copy", [b_ps_tr], [b_G[t]],
                out=G_ap(t), in_=ps_tr[:, 0:1024].rearrange("p (a b) -> p a b", a=8))
            if s == 0 and t == 0:
                dump("xt0", SCR[:, 0:1024], [b_xt[0]])
                dump("stat0", stat[:, 0:8], [b_stat])
                dump("lt0", dtile[:, :], [b_lt[0]])
        dump("ln1_fm", G_ar[:, 0:4096], list(b_G))

    def load_group_weights(g):
        S.dma("qsp", lambda g=g: nc.sync.dma_start(
            out=qkl_bf[:, :], in_=qklb[g, :, :, :].rearrange("p a b -> p (a b)")),
            reads=[b_wcache["qklb"]], writes=[b_qkl])
        S.dma("qsp", lambda g=g: nc.sync.dma_start(
            out=kvv_bf[:, :], in_=kvvb[g, :, :, :].rearrange("p a b -> p (a b)")),
            reads=[b_wcache["kvvb"]], writes=[b_kvv])

    def fm_proj(g):
        for which, wcol, dst, bdst in ((0, 0, qf, b_qf), (1, 128, klf, b_klf)):
            for tq in range(4):
                ps = ps_q[tq]
                for m in range(8):
                    mm(ps[:, 0:512],
                       qkl_bf[:, m * 256 + wcol:m * 256 + wcol + 128],
                       G_ar[:, m * 2048 + tq * 512:m * 2048 + (tq + 1) * 512],
                       start=(m == 0), stop=(m == 7),
                       reads=[b_qkl] + [b_G[tq * 4 + i] for i in range(4)],
                       writes=[b_psq[tq]] if m in (0, 7) else [])
                act("copy", [b_psq[tq]], [bdst],
                    out=dst[:, tq * 512:(tq + 1) * 512], in_=ps[:, 0:512])

    def tm_proj(g):
        import os as _os2
        TMLVL = int(_os2.environ.get("KK_TM", "5"))
        for t in range(NT):
            sl = t % 4
            ps = ps_q[sl]
            slc = 0
            for m in range(8):
                mm(ps[:, slc:slc + 384],
                   G_ar[:, m * 2048 + t * 128:m * 2048 + (t + 1) * 128],
                   kvv_bf[:, m * 384:(m + 1) * 384],
                   start=(m == 0), stop=(m == 7),
                   reads=[b_kvv, b_G[t]],
                   writes=[b_psq[sl]] if m in (0, 7) else [])
            if TMLVL >= 2:
                act("copy", [b_psq[sl]], [b_ktm],
                    out=ktm[:, t * 128:(t + 1) * 128], in_=ps[:, slc:slc + 128])
            if TMLVL == 6:
                src_ap = (ps[:, slc:slc + 128]
                          if _os.environ.get("KK_TSRC", "ps") == "ps"
                          else ktm[:, t * 128:(t + 1) * 128])
                src_b = b_psq[sl] if _os.environ.get("KK_TSRC", "ps") == "ps" else b_ktm
                ts_op(fscr[:, (t % 4) * 128:(t % 4) * 128 + 128], src_ap,
                      0.0, None, ALU.min, None, [src_b], [b_fscr])
                dve("tensor_copy", [b_fscr], [b_fktm],
                    out=fktm[:, t * 128:(t + 1) * 128],
                    in_=fscr[:, (t % 4) * 128:(t % 4) * 128 + 128])
            elif TMLVL == 7:
                fsl = fscr[:, (t % 4) * 128:(t % 4) * 128 + 128]
                ts_op(fsl, ps[:, slc:slc + 128], 0.0, None, ALU.min, None,
                      [b_psq[sl]], [b_fscr])
                act("activation", [b_fscr], [b_fscr], out=fsl, in_=fsl, func=AF.Exp,
                    bias=0.0, scale=1.0)
                dve("tensor_copy", [b_fscr], [b_fktm],
                    out=fktm[:, t * 128:(t + 1) * 128], in_=fsl)
            elif TMLVL >= 3:
                phi3(ps[:, slc:slc + 128], b_psq[sl],
                     fktm[:, t * 128:(t + 1) * 128], b_fktm, (t % 4) * 128, 128)
            if TMLVL >= 4:
                ts_op(vtm[:, t * 128:(t + 1) * 128], ps[:, slc + 128:slc + 256],
                      0.0, None, ALU.add, None, [b_psq[sl]], [b_vtm])
            if TMLVL >= 5:
                act("copy", [b_psq[sl]], [b_vltm],
                    out=vltm[:, t * 128:(t + 1) * 128], in_=ps[:, slc + 256:slc + 384])

    def transpose_2048(srcap, b_src, dstap, b_dst):
        for t in range(NT):
            tr(ps_tr[:, (t % 8) * 128:(t % 8) * 128 + 128],
               srcap[:, t * 128:(t + 1) * 128],
               [b_src], [b_ps_tr] if t % 8 in (0, 7) else [])
            if t % 8 == 7:
                act("copy", [b_ps_tr], [b_dst],
                    out=dstap[:, (t - 7) * 128:(t + 1) * 128], in_=ps_tr[:, 0:1024])

    def local_attn(g):
        LOC = int(_os.environ.get("KK_LOC", "5"))

        def emit_pv(qb):
            sl = (qb % 2) * 512
            for h in (0, 1):
                pso = ps_q[2 * h + qb // 8]
                bq = b_psq[2 * h + qb // 8]
                col = (qb % 8) * 64
                bank_first = qb % 8 == 0
                bank_last = qb % 8 == 7
                mm(pso[:, col:col + 64],
                   ptil[:, sl + h * 256:sl + h * 256 + 128],
                   vltm[:, qb * 128 + 64 * h:qb * 128 + 64 * h + 64],
                   start=bank_first, stop=(bank_last and qb == 0),
                   reads=[b_ptil2[qb % 2], b_vltm],
                   writes=[bq] if bank_first else [])
                mm(ps_nrm[:, h * 16 + qb:h * 16 + qb + 1],
                   ptil[:, sl + h * 256:sl + h * 256 + 128], onesb[:, 0:1],
                   start=(qb == 0 and h == 0), stop=False,
                   reads=[b_ptil2[qb % 2], b_consts],
                   writes=[b_ps_nrm] if (qb == 0 and h == 0) else [])
                if qb > 0:
                    mm(pso[:, col:col + 64],
                       ptil[:, sl + h * 256 + 128:sl + h * 256 + 256],
                       vltm[:, (qb - 1) * 128 + 64 * h:(qb - 1) * 128 + 64 * h + 64],
                       start=False, stop=bank_last,
                       reads=[b_ptil2[qb % 2], b_vltm],
                       writes=[bq] if bank_last else [])
                    mm(ps_nrm[:, h * 16 + qb:h * 16 + qb + 1],
                       ptil[:, sl + h * 256 + 128:sl + h * 256 + 256], onesb[:, 0:1],
                       start=False, stop=(qb == 15 and h == 1),
                       reads=[b_ptil2[qb % 2], b_consts],
                       writes=[b_ps_nrm] if (qb == 15 and h == 1) else [])

        for qb in range(NT):
            mms = []
            for h in (0, 1):
                base = (ps_A, trB)[h]
                bb = (b_ps_A, b_trB2[0])[h]
                mms.append((h, bb, base[:, 0:128],
                            klf[64 * h:64 * h + 64, qb * 128:(qb + 1) * 128]))
                if qb > 0:
                    bb2 = (b_ps_A, b_trB2[1])[h]
                    mms.append((h, bb2, base[:, 128:256],
                                klf[64 * h:64 * h + 64, (qb - 1) * 128:qb * 128]))
            nmm = len(mms)
            for i, (h, bb, oap, lap) in enumerate(mms):
                first = i == 0 or (qb > 0 and i == 2) or (qb == 0 and i == 1)
                last = (qb > 0 and i in (1, 3)) or (qb == 0)
                mm(oap, lap, qf[64 * h:64 * h + 64, qb * 128:(qb + 1) * 128],
                   start=first, stop=last,
                   reads=[b_klf, b_qf] + ([b_ps_tr] if h == 1 and first else []),
                   writes=[bb] if (first or last) else [])
            sl = (qb % 2) * 512
            if LOC < 2:
                continue
            if qb == 0:
                act("activation", [b_ps_A], [b_ptil2[0]], out=ptil[:, sl:sl + 128],
                    in_=ps_A[:, 0:128], func=AF.Exp, bias=0.0, scale=0.125)
                act("activation", [b_trB2[0], b_ps_tr], [b_ptil2[0]],
                    out=ptil[:, sl + 256:sl + 384],
                    in_=trB[:, 0:128], func=AF.Exp, bias=0.0, scale=0.125)
            else:
                act("activation", [b_ps_A], [b_ptil2[qb % 2]], out=ptil[:, sl:sl + 256],
                    in_=ps_A[:, 0:256], func=AF.Exp, bias=0.0, scale=0.125)
                act("activation", [b_trB2[0], b_trB2[1], b_ps_tr], [b_ptil2[qb % 2]],
                    out=ptil[:, sl + 256:sl + 512],
                    in_=trB[:, 0:256], func=AF.Exp, bias=0.0, scale=0.125)
            if LOC < 3:
                continue
            dve("tensor_tensor", [b_ptil2[qb % 2], b_consts], [b_ptil2[qb % 2]],
                out=ptil[:, sl:sl + 512].rearrange("p (a b) -> p a b", a=2)[:, :, 0:128],
                in0=ptil[:, sl:sl + 512].rearrange("p (a b) -> p a b", a=2)[:, :, 0:128],
                in1=mloc0[:, :].rearrange("p (a b) -> p a b", a=2), op=ALU.mult)
            if qb > 0:
                dve("tensor_tensor", [b_ptil2[qb % 2], b_consts], [b_ptil2[qb % 2]],
                    out=ptil[:, sl:sl + 512].rearrange("p (a b) -> p a b", a=2)[:, :, 128:256],
                    in0=ptil[:, sl:sl + 512].rearrange("p (a b) -> p a b", a=2)[:, :, 128:256],
                    in1=mlocn[:, :].rearrange("p (a b) -> p a b", a=2), op=ALU.mult)
            if LOC >= 4 and qb >= 1:
                emit_pv(qb - 1)
        if LOC >= 4:
            emit_pv(15)
        if LOC < 5:
            return
        act("activation", [b_ps_nrm], [b_stat], out=stat[:, 16:48],
            in_=ps_nrm[:, 0:32], func=AF.Ln, bias=0.0, scale=1.0)
        act("activation", [b_stat], [b_stat], out=stat[:, 16:48],
            in_=stat[:, 16:48], func=AF.Exp, bias=0.0, scale=-1.0)
        for h in (0, 1):
            hh = 2 * g + h
            for qb in range(16):
                ts_op(H_ar[:, hh * 1024 + qb * 64:hh * 1024 + (qb + 1) * 64],
                      ps_q[2 * h + qb // 8][:, (qb % 8) * 64:(qb % 8) * 64 + 64],
                      stat[:, 16 + 16 * h + qb:17 + 16 * h + qb], None, ALU.mult,
                      None, [b_psq[2 * h + qb // 8], b_stat], [b_lstrip[hh]])

    def lin_pass(g, p):
        vv, b_vv = (vtm, b_vtm) if p == 0 else (vlvl, b_vlvl)

        def intra(cc):
            for h in (0, 1):
                pso = ps_q[2 * h + cc // 8]
                bq = b_psq[2 * h + cc // 8]
                col = (cc % 8) * 64
                mm(pso[:, col:col + 64],
                   atil[:, (cc % 3) * 256 + h * 128:(cc % 3) * 256 + (h + 1) * 128],
                   vv[:, cc * 128 + 64 * h:cc * 128 + 64 * h + 64],
                   start=(cc in (0, 8)), stop=False,
                   reads=[b_atil4[cc % 3], b_vv],
                   writes=[bq] if cc in (0, 8) else [])
                mm(ps_nrm[:, h * 16 + cc:h * 16 + cc + 1],
                   atil[:, (cc % 3) * 256 + h * 128:(cc % 3) * 256 + (h + 1) * 128],
                   onesb[:, 0:1],
                   start=(cc == 0 and h == 0), stop=False,
                   reads=[b_atil4[cc % 3], b_consts],
                   writes=[b_ps_nrm] if (cc == 0 and h == 0) else [])

        def inter_batch(half):
            lo = max(1, half * 8)
            for h in (0, 1):
                if h == 1:
                    S.pe_fence()
                for cc in range(lo, half * 8 + 8):
                    pso = ps_q[2 * h + half]
                    bq = b_psq[2 * h + half]
                    col = (cc % 8) * 64
                    sn = snap[64 * h:64 * h + 64, 65 * (cc - 1):65 * (cc - 1) + 65]
                    last = cc == half * 8 + 7
                    mm(pso[:, col:col + 64],
                       qf[64 * h:64 * h + 64, cc * 128:(cc + 1) * 128],
                       sn[:, 0:64],
                       start=False, stop=last,
                       reads=[b_qf, b_snap],
                       writes=[bq] if last else [])
                    mm(ps_nrm[:, h * 16 + cc:h * 16 + cc + 1],
                       qf[64 * h:64 * h + 64, cc * 128:(cc + 1) * 128],
                       sn[:, 64:65],
                       start=False, stop=(cc == 15 and h == 1),
                       reads=[b_qf, b_snap],
                       writes=[b_ps_nrm] if (cc == 15 and h == 1) else [])
                if h == 1:
                    S.pe_fence()

        for c in range(16):
            if c < 15:
                for h in (0, 1):
                    st = (c == 0 and h == 0)
                    mm(ps_state[0:64, 65 * h:65 * h + 64],
                       fktm[:, c * 128 + 64 * h:c * 128 + 64 * h + 64],
                       vv[:, c * 128 + 64 * h:c * 128 + 64 * h + 64],
                       start=st, stop=False, skip=(c > 0),
                       reads=[b_fktm, b_vv],
                       writes=[b_ps_state] if h == 0 else [])
                    mm(ps_state[0:64, 65 * h + 64:65 * h + 65],
                       fktm[:, c * 128 + 64 * h:c * 128 + 64 * h + 64],
                       onesb[:, 0:1],
                       start=False, stop=(h == 1), skip=(c > 0),
                       reads=[b_fktm, b_consts],
                       writes=[b_ps_state] if h == 1 else [])
                act("copy", [b_ps_state], [b_snap],
                    out=snap[0:64, 65 * c:65 * c + 65], in_=ps_state[0:64, 0:65])
                act("copy", [b_ps_state], [b_snap],
                    out=snap[64:128, 65 * c:65 * c + 65], in_=ps_state[0:64, 65:130])
            for h in (0, 1):
                base = (ps_A, trB)[h]
                bb = (b_psA2[c % 2], b_trB2[c % 2])[h]
                mm(base[:, (c % 2) * 128:(c % 2) * 128 + 128],
                   fkf[64 * h:64 * h + 64, c * 128:(c + 1) * 128],
                   qf[64 * h:64 * h + 64, c * 128:(c + 1) * 128],
                   start=True, stop=True,
                   reads=[b_fkf, b_qf] + ([b_ps_tr] if h == 1 else []),
                   writes=[bb])
            dve("tensor_tensor", [b_psA2[c % 2], b_consts], [b_atil4[c % 3]],
                out=atil[:, (c % 3) * 256:(c % 3) * 256 + 128],
                in0=ps_A[:, (c % 2) * 128:(c % 2) * 128 + 128],
                in1=mlin[:, 0:128], op=ALU.mult)
            dve("tensor_tensor", [b_trB2[c % 2], b_ps_tr, b_consts],
                [b_atil4[c % 3]],
                out=atil[:, (c % 3) * 256 + 128:(c % 3) * 256 + 256],
                in0=trB[:, (c % 2) * 128:(c % 2) * 128 + 128],
                in1=mlin[:, 0:128], op=ALU.mult)
            if c >= 1:
                intra(c - 1)
            if c == 8:
                inter_batch(0)
        intra(15)
        inter_batch(1)
        ts_op(stat[:, 16:48], ps_nrm[:, 0:32], float(EPS_LA), None, ALU.max, None,
              [b_ps_nrm], [b_stat])
        act("activation", [b_stat], [b_stat], out=stat[:, 16:48],
            in_=stat[:, 16:48], func=AF.Ln, bias=0.0, scale=1.0)
        if p == 0:
            act("activation", [b_stat], [b_stat], out=stat[:, 16:48],
                in_=stat[:, 16:48], func=AF.Exp, bias=0.0, scale=-1.0)
        else:
            act("activation", [b_stat, b_sw], [b_stat], out=stat[:, 16:48],
                in_=stat[:, 16:48], func=AF.Exp, bias=sw_sb[:, 11 + p:12 + p],
                scale=-1.0)
        for h in (0, 1):
            for cc in range(16):
                pslice = ps_q[2 * h + cc // 8][:, (cc % 8) * 64:(cc % 8) * 64 + 64]
                rsc = stat[:, 16 + 16 * h + cc:17 + 16 * h + cc]
                if p == 0:
                    ts_op(acc[:, h * 1024 + cc * 64:h * 1024 + (cc + 1) * 64],
                          pslice, rsc, None, ALU.mult, None,
                          [b_psq[2 * h + cc // 8], b_stat], [b_acc])
                else:
                    ts_op(tmp1[:, cc * 64:(cc + 1) * 64], pslice, rsc, None,
                          ALU.mult, None, [b_psq[2 * h + cc // 8], b_stat],
                          [b_tmp1])
            if p > 0:
                dve("tensor_add", [b_tmp1, b_acc], [b_acc],
                    out=acc[:, h * 1024:(h + 1) * 1024],
                    in0=acc[:, h * 1024:(h + 1) * 1024], in1=tmp1[:, :])

    def haar_prep(g, lvl):
        for which, (srcap, b_src, fmdst, b_fmdst) in enumerate(
                ((ktm, b_ktm, klf, b_klf), (vtm, b_vtm, bmvf, b_bmvf))):
            for t in range(NT):
                sl = t % 8
                pq = ps_q[sl // 4]
                bq = b_psq[sl // 4]
                mm(pq[:, (sl % 4) * 128:(sl % 4 + 1) * 128],
                   lbt[:, lvl * 128:(lvl + 1) * 128],
                   srcap[:, t * 128:(t + 1) * 128],
                   start=True, stop=True,
                   reads=[b_consts, b_src],
                   writes=[bq] if sl % 4 in (0, 3) else [])
                if sl % 4 == 3:
                    if which == 0:
                        ts_op(vltm[:, (t - 3) * 128:(t + 1) * 128],
                              pq[:, 0:512], lenr[:, lvl:lvl + 1], None,
                              ALU.mult, None, [bq, b_consts], [b_vltm])
                    else:
                        act("activation", [bq, b_consts], [b_vltm],
                            out=vltm[:, (t - 3) * 128:(t + 1) * 128],
                            in_=pq[:, 0:512], func=AF.Copy,
                            scale=lenr[:, lvl:lvl + 1], bias=0.0)
            transpose_2048(vltm, b_vltm, fmdst, b_fmdst)
        for which, (fmsrc, b_fm, wcol) in enumerate(
                ((klf, b_klf, lvl * 2), (bmvf, b_bmvf, lvl * 2 + 1))):
            for t in range(NT):
                sl = t % 4
                for h in (0, 1):
                    pq = ps_q[2 + h]
                    bq = b_psq[2 + h]
                    mm(pq[:, sl * 128:sl * 128 + 64],
                       fmsrc[64 * h:64 * h + 64, t * 128:(t + 1) * 128],
                       haarw[64 * h:64 * h + 64, wcol * 64:(wcol + 1) * 64],
                       start=True, stop=True,
                       reads=[b_fm, b_consts],
                       writes=[bq] if sl in (0, 3) else [])
                if sl == 3:
                    for h in (0, 1):
                        pq = ps_q[2 + h]
                        bq = b_psq[2 + h]
                        src_ap = pq[:, 0:512].rearrange("p (a b) -> p a b", a=4)[:, :, 0:64]
                        dst_ap = (fktm if which == 0 else vlvl)[
                            :, (t - 3) * 128:(t + 1) * 128].rearrange(
                            "p (a b) -> p a b", a=4)[:, :, 64 * h:64 * h + 64]
                        if which == 0:
                            fs_ap = fscr[:, h * 256:h * 256 + 256].rearrange(
                                "p (a b) -> p a b", a=4)
                            phi3(src_ap, bq, dst_ap, b_fktm, 0, 256, fs_ap=fs_ap)
                        else:
                            act("copy", [bq], [b_vlvl], out=dst_ap, in_=src_ap)
        transpose_2048(fktm, b_fktm, fkf, b_fkf)

    b_wg = B("wg"); b_wo_ = B("wo"); b_x2 = B("x2")
    b_w1s = [B("w1s0"), B("w1s1")]; b_w2s = [B("w2s0"), B("w2s1")]
    b_h1s = [B("h1s0"), B("h1s1")]; b_w3s = [B("w3s0"), B("w3s1")]

    def wg_tile(gc, fc):
        if gc < 4:
            return WB[:, (gc * 16 + fc) * 128:(gc * 16 + fc + 1) * 128]
        tens = (qf, klf, ktm, fktm)[gc - 4]
        return tens[:, fc * 128:(fc + 1) * 128]

    def gate_phase(s, g_dumps=False):
        rebind([b_wg], [b_wg, b_wo_] + b_h1s)
        rebind([b_qf, b_klf, b_ktm, b_fktm],
               [b_qf, b_klf, b_ktm, b_fktm])
        S.dma("qsp", lambda: nc.sync.dma_start(
            out=WB[:, 0:8192],
            in_=wgb_d[:, 0:4, :, :].rearrange("p a b c -> p (a b c)")),
            reads=[b_wcache["wgb_d"]], writes=[b_wg])
        for gi, tens in enumerate((qf, klf, ktm, fktm)):
            S.dma("qsp", lambda t_=tens, gi=gi: nc.sync.dma_start(
                out=t_[:, :],
                in_=wgb_d[:, 4 + gi, :, :].rearrange("p a b -> p (a b)")),
                reads=[b_wcache["wgb_d"]],
                writes=[(b_qf, b_klf, b_ktm, b_fktm)[gi]])
        pass
        for t in range(NT):
            sl = t % 2
            dve("tensor_tensor", [b_lstrip[i] for i in range(16)] +
                [b_gstrip[i] for i in range(16)], [b_lt[0]],
                out=dtile[:, :].rearrange("p (a b) -> p a b", a=16),
                in0=strip_ap(False, t), in1=strip_ap(True, t), op=ALU.subtract)
            for fc in range(8):
                tr(ps_tr[:, fc * 128:(fc + 1) * 128], dtile[:, fc * 128:(fc + 1) * 128],
                   [b_lt[0]], [b_ps_tr] if fc in (0, 7) else [])
            act("copy", [b_ps_tr], [b_lt[1]], out=dfm[:, :], in_=ps_tr[:, 0:1024])
            for gc in range(8):
                for fc in range(16):
                    rhs = (G_ar[:, fc * 2048 + t * 128:fc * 2048 + (t + 1) * 128]
                           if fc < 8 else dfm[:, (fc - 8) * 128:(fc - 7) * 128])
                    rb = [b_G[t]] if fc < 8 else [b_lt[1]]
                    wgb_src = ([b_wg] if gc < 4 else
                               [(b_qf, b_klf, b_ktm, b_fktm)[gc - 4]])
                    pq = ps_q[2 + gc // 4]
                    bq = b_psq[2 + gc // 4]
                    mm(pq[:, (gc % 4) * 128:(gc % 4 + 1) * 128],
                       wg_tile(gc, fc),
                       rhs, start=(fc == 0), stop=(fc == 15),
                       reads=wgb_src + rb,
                       writes=[bq] if (fc == 0 and gc % 4 == 0) or (fc == 15 and gc % 4 == 3) else [])
            for gc in range(8):
                act("activation", [b_psq[2 + gc // 4], b_consts], [b_gatet],
                    out=gatet[:, gc * 128:(gc + 1) * 128],
                    in_=ps_q[2 + gc // 4][:, (gc % 4) * 128:(gc % 4 + 1) * 128],
                    func=AF.Silu, bias=bg_sb[:, gc:gc + 1], scale=1.0)
            for gc in range(8):
                mm(ps_nrm[:, 0:1], gatet[:, gc * 128:(gc + 1) * 128],
                   wgo_sb[:, gc:gc + 1], start=(gc == 0), stop=(gc == 7),
                   reads=[b_gatet, b_consts],
                   writes=[b_ps_nrm] if gc in (0, 7) else [])
            act("activation", [b_ps_nrm, b_consts], [b_alp],
                out=alph[:, t:t + 1], in_=ps_nrm[:, 0:1], func=AF.Sigmoid,
                bias=bgo_sb[:, :], scale=1.0)
            act("activation", [b_alp, b_consts], [b_alp],
                out=alph[:, 16 + t:17 + t], in_=alph[:, t:t + 1], func=AF.Copy,
                bias=-1.0, scale=1.0)
            S.op("dve", lambda t=t: nc.vector.scalar_tensor_tensor(
                out=gatet[:, :].rearrange("p (a b) -> p a b", a=16),
                in0=dtile[:, :].rearrange("p (a b) -> p a b", a=16),
                scalar=alph[:, 16 + t:17 + t], in1=strip_ap(False, t),
                op0=ALU.mult, op1=ALU.add),
                reads=[b_lt[0], b_alp] + [b_lstrip[i] for i in range(16)],
                writes=[b_gatet])
            for fc in range(8):
                tr(ps_tr[:, fc * 128:(fc + 1) * 128], gatet[:, fc * 128:(fc + 1) * 128],
                   [b_gatet], [b_ps_tr] if fc in (0, 7) else [])
            act("copy", [b_ps_tr], [b_G[t]],
                out=G_ap(t), in_=ps_tr[:, 0:1024].rearrange("p (a b) -> p a b", a=8))
        if g_dumps:
            dump("alpha", alph[:, 0:16], [b_alp])
            dump("mixed_fm", G_ar[:, 0:2048], list(b_G))

    def wo_phase(s, g_dumps=False):
        rebind([b_wo_], [b_wg])
        S.dma("qsp", lambda: nc.sync.dma_start(
            out=WB[:, 0:8192], in_=wob_d.rearrange("p a b -> p (a b)")),
            reads=[b_wcache["wob_d"]], writes=[b_wo_])
        rebind([b_ptil2[0], b_ptil2[1]], b_ptil2)
        S.dma("qsp", lambda: nc.sync.dma_start(out=SCR[:, 3072:4096], in_=c_bo),
              writes=[b_trash])
        dve("tensor_copy", [b_trash], [b_ptil2[0]], out=ptil[:, 0:1024],
            in_=SCR[:, 3072:4096])
        rebind(b_xt + [b_ao, b_trash], b_xt + [b_ao, b_trash])
        for t in range(NT):
            sl = t % 2
            for m in range(8):
                for hf in (0, 1):
                    mm(ps_q[hf][:, 0:512],
                       G_ar[:, m * 2048 + t * 128:m * 2048 + (t + 1) * 128],
                       WB[:, m * 1024 + hf * 512:m * 1024 + (hf + 1) * 512],
                       start=(m == 0), stop=(m == 7),
                       reads=[b_G[t], b_wo_],
                       writes=[b_psq[hf]] if (m in (0, 7)) else [])
            xt = SCR[:, sl * 1024:(sl + 1) * 1024]
            S.dma("qsp", lambda d=xt, s_=s, t=t: nc.sync.dma_start(
                out=d, in_=x_hbm[s_, t * 128:(t + 1) * 128, :]), writes=[b_xt[sl]])
            ao = SCR[:, 2048:3072]
            for hf in (0, 1):
                dve("tensor_tensor", [b_psq[hf], b_ptil2[0]], [b_ao],
                    out=SCR[:, 2048 + hf * 512:2048 + (hf + 1) * 512],
                    in0=ps_q[hf][:, 0:512],
                    in1=ptil[:, hf * 512:(hf + 1) * 512], op=ALU.add)
            ln_chain(ao, b_ao, ao, b_ao, (t % 2) * 8)
            dve("tensor_tensor", [b_ao, b_xt[sl]], [b_xt[sl]],
                out=xt, in0=ao, in1=xt, op=ALU.add)
            S.dma("qpool", lambda d=t, s_=s, src=xt: nc.gpsimd.dma_start(
                out=y1d[s_, d, :, :], in_=src), reads=[b_xt[sl]], writes=[b_y1d])
            if g_dumps and t == 0:
                dump("y1_t0", xt, [b_xt[sl]])

    def ffn_phase(s, g_dumps=False):
        rebind(b_h, b_h + [b_lstrip[i] for i in range(16)] +
               [b_gstrip[i] for i in range(16)])
        rebind(b_g1, b_g1 + b_G)
        rebind(b_h1s, [b_wo_] + b_h1s)
        rebind([b_vtm, b_vlvl], [b_vtm, b_vlvl])
        rebind(b_w1s, [b_qkl] + b_w1s)
        rebind(b_w2s, [b_kvv] + b_w2s)
        rebind(b_w3s, [b_klf] + b_w3s)
        rebind(b_y1s + [b_fout], b_xt + [b_ao, b_trash] + b_y1s + [b_fout])
        S.dma("qsp", lambda: nc.sync.dma_start(out=SCR[:, 3072:4096], in_=c_s3),
              writes=[b_trash])
        dve("tensor_copy", [b_trash], [b_ptil2[0]], out=ptil[:, 0:1024],
            in_=SCR[:, 3072:4096])

        def x2_ap(m):
            t_ = (vtm, vlvl)[m // 4]
            mm_ = m % 4
            return t_[:, mm_ * 512:(mm_ + 1) * 512]
        b_x2l = [b_vtm, b_vlvl]
        for tg in range(4):
            pass
            for tt in range(4):
                t = tg * 4 + tt
                ysl = tt % 2
                ys = SCR[:, ysl * 1024:(ysl + 1) * 1024]
                S.dma("qact", lambda d=ys, s_=s, t=t: nc.scalar.dma_start(
                    out=d, in_=y1d[s_, t, :, :]), reads=[b_y1d], writes=[b_y1s[ysl]])
                lt = (dtile, dfm)[ysl]
                ln_chain(ys, b_y1s[ysl], lt[:, :], b_lt[ysl], (tt % 2) * 8)
                for fc in range(8):
                    tr(ps_tr[:, fc * 128:(fc + 1) * 128], lt[:, fc * 128:(fc + 1) * 128],
                       [b_lt[ysl]], [b_ps_tr] if fc in (0, 7) else [])
                S.op("act", lambda tt=tt: nc.scalar.copy(
                    out=vtm[:, :].rearrange("p (a b) -> p a b", a=4)[
                        :, :, tt * 128:(tt + 1) * 128],
                    in_=ps_tr[:, 0:512].rearrange("p (a b) -> p a b", a=4)),
                    reads=[b_ps_tr], writes=[b_vtm])
                S.op("act", lambda tt=tt: nc.scalar.copy(
                    out=vlvl[:, :].rearrange("p (a b) -> p a b", a=4)[
                        :, :, tt * 128:(tt + 1) * 128],
                    in_=ps_tr[:, 512:1024].rearrange("p (a b) -> p a b", a=4)),
                    reads=[b_ps_tr], writes=[b_vlvl])
            for oc in range(32):
                sl = oc % 2
                S.dma("qsp", lambda oc=oc, sl=sl: nc.sync.dma_start(
                    out=qkl_bf[:, sl * 1024:(sl + 1) * 1024],
                    in_=w1b[oc, :, :, :].rearrange("p a b -> p (a b)")),
                    reads=[b_wcache["w1b"]], writes=[b_w1s[sl]])
                S.dma("qsp", lambda oc=oc, sl=sl: nc.sync.dma_start(
                    out=kvv_bf[:, sl * 1024:(sl + 1) * 1024],
                    in_=w2b[oc, :, :, :].rearrange("p a b -> p (a b)")),
                    reads=[b_wcache["w2b"]], writes=[b_w2s[sl]])
                bps = (b_psq[sl * 2], b_psq[sl * 2 + 1])
                for m in range(8):
                    mm(ps_q[sl * 2][:, 0:512],
                       qkl_bf[:, sl * 1024 + m * 128:sl * 1024 + (m + 1) * 128],
                       x2_ap(m),
                       start=(m == 0), stop=(m == 7),
                       reads=[b_w1s[sl], b_x2l[m // 4]],
                       writes=[bps[0]] if m in (0, 7) else [])
                for m in range(8):
                    mm(ps_q[sl * 2 + 1][:, 0:512],
                       kvv_bf[:, sl * 1024 + m * 128:sl * 1024 + (m + 1) * 128],
                       x2_ap(m),
                       start=(m == 0), stop=(m == 7),
                       reads=[b_w2s[sl], b_x2l[m // 4]],
                       writes=[bps[1]] if m in (0, 7) else [])
                act("activation", [bps[0], b_consts], [b_abuf], out=abuf[:, :],
                    in_=ps_q[sl * 2][:, 0:512], func=AF.Silu,
                    scale=s1_sb[:, oc:oc + 1], bias=0.0)
                ts_op(bbuf[:, :], ps_q[sl * 2 + 1][:, 0:512], s2_sb[:, oc:oc + 1], None,
                      ALU.mult, None, [bps[1], b_consts], [b_bbuf])
                dve("tensor_tensor", [b_abuf, b_bbuf], [b_h[oc]],
                    out=H_ar[:, oc * 512:(oc + 1) * 512], in0=abuf[:, :],
                    in1=bbuf[:, :], op=ALU.mult)
            for oc in range(32):
                sl = oc % 2
                S.dma("qsp", lambda oc=oc, sl=sl: nc.sync.dma_start(
                    out=WB[:, sl * 4096:(sl + 1) * 4096],
                    in_=h1b[oc, :, :, :].rearrange("p a b -> p (a b)")),
                    reads=[b_wcache["h1b"]], writes=[b_h1s[sl]])
                qi = oc % 4
                bps = b_psq[qi]
                for m in range(32):
                    mm(ps_q[qi][:, 0:512],
                       WB[:, sl * 4096 + m * 128:sl * 4096 + (m + 1) * 128],
                       H_ar[:, m * 512:(m + 1) * 512],
                       start=(m == 0), stop=(m == 31),
                       reads=[b_h1s[sl]] + ([b_h[i] for i in range(32)] if m == 0 else []),
                       writes=[bps] if m in (0, 31) else [])
                act("activation", [bps, b_consts], [b_g1[oc]],
                    out=G_ar[:, oc * 512:(oc + 1) * 512],
                    in_=ps_q[qi][:, 0:512], func=AF.Gelu,
                    scale=s1h_sb[:, oc:oc + 1], bias=0.0)
            for oc in range(32):
                sl = oc % 2
                S.dma("qsp", lambda oc=oc, sl=sl: nc.sync.dma_start(
                    out=WB[:, sl * 4096:(sl + 1) * 4096],
                    in_=h2b[oc, :, :, :].rearrange("p a b -> p (a b)")),
                    reads=[b_wcache["h2b"]], writes=[b_h1s[sl]])
                qi = oc % 4
                bps = b_psq[qi]
                for m in range(32):
                    mm(ps_q[qi][:, 0:512],
                       WB[:, sl * 4096 + m * 128:sl * 4096 + (m + 1) * 128],
                       G_ar[:, m * 512:(m + 1) * 512],
                       start=(m == 0), stop=(m == 31),
                       reads=[b_h1s[sl]] + ([b_g1[i] for i in range(32)] if m == 0 else []),
                       writes=[bps] if m in (0, 31) else [])
                act("activation", [bps, b_consts], [b_h[oc]],
                    out=H_ar[:, oc * 512:(oc + 1) * 512],
                    in_=ps_q[qi][:, 0:512], func=AF.Gelu,
                    scale=s2h_sb[:, oc:oc + 1], bias=0.0)
            pass
            for half in range(2):
                for m in range(32):
                    sl = m % 2
                    S.dma("qsp", lambda m=m, sl=sl: nc.sync.dma_start(
                        out=klf[:, sl * 1024:(sl + 1) * 1024], in_=w3b[:, m, :]),
                        reads=[b_wcache["w3b"]], writes=[b_w3s[sl]])
                    for tt2 in (0, 1):
                        tloc = half * 2 + tt2
                        for hf in (0, 1):
                            qi = tt2 * 2 + hf
                            mm(ps_q[qi][:, 0:512],
                               H_ar[:, m * 512 + tloc * 128:m * 512 + (tloc + 1) * 128],
                               klf[:, sl * 1024 + hf * 512:sl * 1024 + (hf + 1) * 512],
                               start=(m == 0), stop=(m == 31),
                               reads=[b_w3s[sl], b_h[m]],
                               writes=[b_psq[qi]] if m in (0, 31) else [])
                for tt2 in (0, 1):
                    t = tg * 4 + half * 2 + tt2
                    fo = SCR[:, 2048:3072]
                    for hf in (0, 1):
                        dve("tensor_tensor", [b_psq[tt2 * 2 + hf], b_ptil2[0]], [b_fout],
                            out=SCR[:, 2048 + hf * 512:2048 + (hf + 1) * 512],
                            in0=ps_q[tt2 * 2 + hf][:, 0:512],
                            in1=ptil[:, hf * 512:(hf + 1) * 512], op=ALU.mult)
                    ln_chain(fo, b_fout, fo, b_fout, (tt2 % 2) * 8)
                    ysl = tt2
                    ys = SCR[:, ysl * 1024:(ysl + 1) * 1024]
                    S.dma("qact", lambda d=ys, s_=s, t=t: nc.scalar.dma_start(
                        out=d, in_=y1d[s_, t, :, :]), reads=[b_y1d], writes=[b_y1s[ysl]])
                    dve("tensor_tensor", [b_fout, b_y1s[ysl]], [b_y1s[ysl]],
                        out=ys, in0=fo, in1=ys, op=ALU.add)
                    S.dma("qpool", lambda src=ys, s_=s, t=t: nc.gpsimd.dma_start(
                        out=y_hbm[s_, t * 128:(t + 1) * 128, :], in_=src),
                        reads=[b_y1s[ysl]], is_output=True)
                if g_dumps and half == 0 and tg == 0:
                    dump("ffn_h", H_ar[:, 0:1024], [b_h[0], b_h[1]])

    import os as _os
    KK_GROUPS = int(_os.environ.get("KK_GROUPS", "8"))
    KK_LEVELS = int(_os.environ.get("KK_LEVELS", "4"))
    KK_SEQS = int(_os.environ.get("KK_SEQS", "2"))
    KK_PH = _os.environ.get("KK_PHASES", "xgtwf")
    KK_GS = _os.environ.get("KK_GSTAGES", "plqmh")

    def plan_seq(s):
        phase_x(s)
        if "g" not in KK_PH:
            return
        if s == 0:
            plan_precast(2 / 14)
        load_group_weights(0)
        for g in range(KK_GROUPS):
            dbg = (s == 0 and g == 0)
            pass
            if "p" not in KK_GS and "2" not in KK_GS and "3" not in KK_GS:
                continue
            if "p" in KK_GS or "2" in KK_GS:
                fm_proj(g)
            if "p" in KK_GS or "3" in KK_GS:
                tm_proj(g)
            if dbg:
                dump("qf_g0", qf[:, 0:2048], [b_qf])
                dump("klf_g0", klf[:, 0:2048], [b_klf])
                dump("ktm_g0", ktm[:, 0:2048], [b_ktm])
                dump("vtm_g0", vtm[:, 0:2048], [b_vtm])
                dump("vltm_g0", vltm[:, 0:2048], [b_vltm])
                dump("fktm_g0", fktm[:, 0:2048], [b_fktm])
            if g + 1 < KK_GROUPS:
                load_group_weights(g + 1)
            if "p" in KK_GS or "3" in KK_GS:
                transpose_2048(fktm, b_fktm, fkf, b_fkf)
            if dbg:
                dump("fkf_g0", fkf[:, 0:2048], [b_fkf])
            pass
            rebind([b_ps_A], [b_ps_A] + b_psA2)
            if "l" in KK_GS:
                local_attn(g)
                if dbg:
                    dump("ls_g0", H_ar[:, 0:2048], [b_lstrip[0], b_lstrip[1]])
            if "q" in KK_GS:
                for tq in range(4):
                    phi3(qf[:, tq * 512:(tq + 1) * 512], b_qf,
                         qf[:, tq * 512:(tq + 1) * 512], b_qf, 0, 512)
                if dbg:
                    dump("phiq_g0", qf[:, 0:2048], [b_qf])
            rebind(b_psA2, [b_ps_A] + b_psA2)
            if "m" in KK_GS:
                lin_pass(g, 0)
                if dbg:
                    dump("acc_main_g0", acc[:, :], [b_acc])
            for lvl in range(KK_LEVELS if "h" in KK_GS else 0):
                haar_prep(g, lvl)
                if dbg and lvl == 0:
                    dump("klvl0_fktm", fktm[:, 0:2048], [b_fktm])
                    dump("vlvl0", vlvl[:, 0:2048], [b_vlvl])
                lin_pass(g, lvl + 1)
            if "m" in KK_GS:
                dve("tensor_copy", [b_acc], [b_gstrip[2 * g], b_gstrip[2 * g + 1]],
                    out=H_ar[:, 16384 + 2 * g * 1024:16384 + (2 * g + 2) * 1024],
                    in_=acc[:, :])
                if dbg:
                    dump("gs_g0", H_ar[:, 16384:18432], [b_gstrip[0], b_gstrip[1]])
        if "t" in KK_PH:
            gate_phase(s, g_dumps=(s == 0))
        if "w" in KK_PH:
            wo_phase(s, g_dumps=(s == 0))
        if "f" in KK_PH:
            ffn_phase(s, g_dumps=(s == 0))

    if _os.environ.get("KK_NOPC", "0") != "1":
        plan_precast(1.0)
    plan_seq(0)
    if KK_SEQS > 1:
        plan_seq(1)
    S.barrier_wait("sp", S.out_dma_toks)

    with nc.Block() as block:
        S.finalize(nc, block, None)
    stats = S.stats()
    S.close()
    st.close()
    return nc, dumps, stats



def _host_inputs(inp):
    f32 = np.float32
    g = {}
    Wq = inp["L0_attn_Wq"]; Wkl = inp["L0_attn_Wkl"]
    Wk = inp["L0_attn_Wk"]; Wv = inp["L0_attn_Wv"]; Wvl = inp["L0_attn_Wvl"]
    qkl = np.empty((8, 128, 8, 256), f32)
    qkl[:, :, :, 0:128] = Wq.reshape(8, 128, 8, 128).transpose(0, 3, 2, 1)
    qkl[:, :, :, 128:256] = Wkl.reshape(8, 128, 8, 128).transpose(0, 3, 2, 1)
    g["w_qkl"] = qkl
    kvv = np.empty((8, 128, 8, 384), f32)
    kvv[:, :, :, 0:128] = Wk.reshape(8, 128, 8, 128).transpose(0, 3, 2, 1)
    kvv[:, :, :, 128:256] = Wv.reshape(8, 128, 8, 128).transpose(0, 3, 2, 1)
    kvv[:, :, :, 256:384] = Wvl.reshape(8, 128, 8, 128).transpose(0, 3, 2, 1)
    g["w_kvv"] = kvv
    Wg = inp["L0_attn_Wg"]
    g["w_g"] = np.ascontiguousarray(
        Wg.reshape(8, 128, 16, 128).transpose(3, 0, 2, 1))
    g["w_go"] = np.ascontiguousarray(inp["L0_attn_Wgo"].reshape(8, 128).T)
    Wo = inp["L0_attn_Wo"]
    g["w_o"] = np.ascontiguousarray(Wo.T.reshape(8, 128, 1024).transpose(1, 0, 2))
    hw = np.empty((128, 8, 64), f32)
    for lvl in range(4):
        kT = inp["L0_attn_haar_Wk"][lvl].T
        vT = inp["L0_attn_haar_Wv"][lvl].T
        hw[0:64, lvl * 2, :] = kT; hw[64:128, lvl * 2, :] = kT
        hw[0:64, lvl * 2 + 1, :] = vT; hw[64:128, lvl * 2 + 1, :] = vT
    g["w_haar"] = hw
    g["c_haarsc"] = np.tile(inp["L0_attn_haar_scale"].reshape(1, 4), (128, 1)).astype(f32)
    g["c_bg"] = np.ascontiguousarray(inp["L0_attn_bg"].reshape(8, 128).T.astype(f32))
    g["c_bgo"] = np.tile(inp["L0_attn_bgo"].reshape(1, 1), (128, 1)).astype(f32)
    g["c_bo"] = np.tile(inp["L0_attn_bo"].reshape(1, 1024), (128, 1)).astype(f32)
    g["c_s3"] = np.tile(inp["L0_ffn_w3_s"].reshape(1, 1024), (128, 1)).astype(f32)
    g["w_1"] = np.ascontiguousarray(
        inp["L0_ffn_w1_w"].reshape(32, 128, 8, 128).transpose(0, 3, 2, 1))
    g["w_2"] = np.ascontiguousarray(
        inp["L0_ffn_w2_w"].reshape(32, 128, 8, 128).transpose(0, 3, 2, 1))
    g["w_h1"] = np.ascontiguousarray(
        inp["L0_ffn_h1_w"].reshape(32, 128, 32, 128).transpose(0, 3, 2, 1))
    g["w_h2"] = np.ascontiguousarray(
        inp["L0_ffn_h2_w"].reshape(32, 128, 32, 128).transpose(0, 3, 2, 1))
    g["w_3"] = np.ascontiguousarray(
        inp["L0_ffn_w3_w"].T.reshape(32, 128, 1024).transpose(1, 0, 2))
    for nm, key in (("c_s1", "L0_ffn_w1_s"), ("c_s2", "L0_ffn_w2_s"),
                    ("c_s1h", "L0_ffn_h1_s"), ("c_s2h", "L0_ffn_h2_s")):
        g[nm] = np.ascontiguousarray(inp[key].reshape(32, 128).T.astype(f32))
    g["c_ident"] = np.eye(128, dtype=f32).astype(bf16)
    jj, ii = np.meshgrid(np.arange(128), np.arange(128), indexing="ij")
    mlin1 = (ii >= jj).astype(f32)
    g["c_mlin"] = np.tile(mlin1, (1, 2)).astype(bf16)
    mfull = ((ii >= jj) & (ii < jj + 64)).astype(f32)
    mprev = (jj > ii + 64).astype(f32)
    g["c_mloc0"] = np.tile(mfull, (1, 2)).astype(bf16)
    g["c_mlocn"] = np.tile(mprev, (1, 2)).astype(bf16)
    lbt = np.zeros((128, 4, 128), f32)
    lenr = np.zeros((128, 4), f32)
    p_ = np.arange(128)
    for lvl in range(4):
        b = 2 ** (lvl + 1)
        lbt[:, lvl, :] = ((p_[:, None] // b == p_[None, :] // b) &
                          (p_[:, None] <= p_[None, :])).astype(f32)
        lenr[:, lvl] = 1.0 / (p_ % b + 1)
    g["c_lbt"] = lbt.astype(bf16)
    g["c_lenr"] = lenr
    g["c_ones"] = np.ones((128, 128), f32).astype(bf16)
    return g


_CACHE = {}


def kernel(**inputs):
    inputs = {k: np.asarray(v) for k, v in inputs.items()}
    debug_names = _CACHE.get("debug_names", ())
    key = ("prog", tuple(sorted(debug_names)))
    if key not in _CACHE:
        _CACHE[key] = build_program(debug_names)
    nc, dumps, stats = _CACHE[key]
    shared = _host_inputs(inputs)
    x = np.ascontiguousarray(inputs["L0_attn_x"], dtype=np.float32)
    in_maps = []
    for c in range(NCORES):
        m = dict(shared)
        m["x"] = x[c * SEQ_PER_CORE:(c + 1) * SEQ_PER_CORE]
        in_maps.append(m)
    res = run_bass_kernel_spmd(nc, in_maps, core_ids=list(range(NCORES)))
    y = np.concatenate([np.asarray(r["y"], dtype=np.float32)
                        for r in res.results], axis=0)
    _CACHE["last_results"] = res
    _CACHE["last_dumps"] = dumps
    return y


# version 04 (final)
# speedup vs baseline: 1.1096x; 1.1096x over previous
import numpy as np
import ml_dtypes
import concourse.bass as bass
import concourse.mybir as mybir
from concourse.bass_utils import run_bass_kernel_spmd

F32 = mybir.dt.float32
BF16 = mybir.dt.bfloat16
AF = mybir.ActivationFunctionType
ALU = mybir.AluOpType


ENGS = ("pe", "dve", "act", "pool", "sp")
QUEUES = {"qsp": "sp", "qpool": "pool", "qact": "act"}
NLANES = 8
LANE_SPACES = tuple(f"{q}#{i}" for q in QUEUES for i in range(NLANES))
SPACES = ENGS + LANE_SPACES
SPACE_IDX = {s: i for i, s in enumerate(SPACES)}
NSPACE = len(SPACES)


class Buf:
    __slots__ = ("name", "wtoks", "readers", "excl")

    def __init__(self, name, excl=False):
        self.name = name
        self.wtoks = {}
        self.readers = []
        self.excl = excl

    def __repr__(self):
        return f"Buf({self.name})"


class Op:
    __slots__ = ("fn", "waits", "inc", "drain")

    def __init__(self, fn, waits):
        self.fn = fn
        self.waits = waits
        self.inc = False
        self.drain = False


import os as _os_mod
SELF_EDGES = _os_mod.environ.get("KK_SELF", "0") == "1"


class Sched:
    def __init__(self):
        self.ops = {e: [] for e in ENGS}
        self.dmacnt = {q: 0 for q in QUEUES}
        self.clk = {e: [-1] * NSPACE for e in ENGS}
        self.guar = {s: [] for s in SPACES}
        self.need_inc = {e: set() for e in ENGS}
        self.waited_self = {e: -1 for e in ENGS}
        self.out_dma_toks = []

    def buf(self, name, excl=False):
        return Buf(name, excl)

    def _collect(self, reads, writes, eng=None):
        need = []
        for b in reads:
            if b is None:
                continue
            need.extend(b.wtoks.items())
            if b.excl:
                need.extend(t for t in b.readers if t[0] != eng)
        for b in writes:
            if b is None:
                continue
            need.extend(b.wtoks.items())
            need.extend(b.readers)
        return need

    def _commit(self, tok, reads, writes):
        for b in reads:
            if b is not None:
                b.readers.append(tok)
        for b in writes:
            if b is not None:
                b.wtoks = {tok[0]: tok[1]}
                b.readers = []

    def op(self, eng, fn, reads=(), writes=(), drain=False):
        idx = len(self.ops[eng])
        myclk = self.clk[eng]
        waits = []
        for (s, i) in self._collect(reads, writes, eng):
            if s == eng and eng == "pe":
                continue
            if s == eng:
                if not SELF_EDGES or i <= self.waited_self[eng]:
                    continue
                waits.append((s, i))
                self.waited_self[eng] = i
                self.need_inc[s].add(i)
                continue
            if myclk[SPACE_IDX[s]] >= i:
                continue
            waits.append((s, i))
            g = self.guar[s][i]
            for k in range(NSPACE):
                if g[k] > myclk[k]:
                    myclk[k] = g[k]
            if s in ENGS:
                self.need_inc[s].add(i)
        myclk[SPACE_IDX[eng]] = idx
        self.guar[eng].append(tuple(myclk))
        o = Op(fn, waits)
        o.drain = drain
        self.ops[eng].append(o)
        self._commit((eng, idx), reads, writes)
        return (eng, idx)

    def dma(self, queue, fn, reads=(), writes=(), is_output=False):
        eng = QUEUES[queue]
        idx = len(self.ops[eng])
        d = self.dmacnt[queue]
        self.dmacnt[queue] = d + 1
        lane_space = f"{queue}#{d % NLANES}"
        use = d // NLANES
        myclk = self.clk[eng]
        waits = []
        need0 = [(lane_space, use - 1)] if use >= 1 else []
        for (s, i) in need0 + self._collect(reads, writes, eng):
            if s == eng and eng == "pe":
                continue
            if s == eng:
                if not SELF_EDGES or i <= self.waited_self[eng]:
                    continue
                waits.append((s, i))
                self.waited_self[eng] = i
                self.need_inc[s].add(i)
                continue
            if myclk[SPACE_IDX[s]] >= i:
                continue
            waits.append((s, i))
            g = self.guar[s][i]
            for k in range(NSPACE):
                if g[k] > myclk[k]:
                    myclk[k] = g[k]
            if s in ENGS:
                self.need_inc[s].add(i)
        myclk[SPACE_IDX[eng]] = idx
        self.guar[eng].append(tuple(myclk))
        qclk = list(myclk)
        qclk[SPACE_IDX[lane_space]] = use
        self.guar[lane_space].append(tuple(qclk))
        op = Op(fn, waits)
        op.inc = ("dma", queue, d)
        self.ops[eng].append(op)
        self._commit((lane_space, use), reads, writes)
        if is_output:
            self.out_dma_toks.append((lane_space, use))
        return (lane_space, use)

    def pe_fence(self):
        idx = len(self.ops["pe"]) - 1
        if idx < 0:
            return
        self.need_inc["pe"].add(idx)
        o = Op(None, [("pe", idx)])
        self.ops["pe"].append(o)
        myclk = self.clk["pe"]
        myclk[SPACE_IDX["pe"]] = idx + 1
        self.guar["pe"].append(tuple(myclk))

    def barrier_wait(self, eng, toks):
        fake = Buf("_barrier")
        fake.wtoks = {}
        waits = []
        myclk = self.clk[eng]
        for (s, i) in toks:
            if s == eng:
                continue
            if myclk[SPACE_IDX[s]] >= i:
                continue
            waits.append((s, i))
            g = self.guar[s][i]
            for k in range(NSPACE):
                if g[k] > myclk[k]:
                    myclk[k] = g[k]
            if s in ENGS:
                self.need_inc[s].add(i)
        idx = len(self.ops[eng])
        myclk[SPACE_IDX[eng]] = idx
        self.guar[eng].append(tuple(myclk))
        self.ops[eng].append(Op(None, waits))

    def finalize(self, nc, block, handles):
        rank = {}
        for e in ENGS:
            marks = sorted(self.need_inc[e])
            rmap = {}
            r = 0
            for m in marks:
                r += 1
                rmap[m] = r
            rank[e] = rmap
        sems = {}
        import contextlib
        self._stack = contextlib.ExitStack()
        for s in ENGS:
            sems[s] = self._stack.enter_context(nc.semaphore(f"sem_{s}"))
        lanes = {}
        for q in QUEUES:
            lanes[q] = [self._stack.enter_context(nc.semaphore(f"sem_{q}{i}"))
                        for i in range(NLANES)]
        self.sems = sems
        self.lanes = lanes

        def run_engine(e, ehandle, lo, hi):
            for i in range(lo, hi):
                op = self.ops[e][i]
                if op.drain:
                    ehandle.drain()
                for (s, j) in op.waits:
                    if s in ENGS:
                        ehandle.wait_ge(sems[s], rank[s][j])
                    else:
                        q, lane = s.split("#")
                        ehandle.wait_ge(lanes[q][int(lane)], 16 * (j + 1))
                if op.fn is None:
                    continue
                inst = op.fn()
                if isinstance(op.inc, tuple) and op.inc[0] == "dma":
                    q, dd = op.inc[1], op.inc[2]
                    inst.then_inc(lanes[q][dd % NLANES], 16)
                    if i in self.need_inc[e]:
                        inst.then_inc(sems[e], 1)
                elif i in self.need_inc[e]:
                    inst.then_inc(sems[e], 1)

        section = {
            "pe": block.tensor, "dve": block.vector, "act": block.scalar,
            "pool": block.gpsimd, "sp": block.sync,
        }
        import os
        if os.environ.get("KK_LIST"):
            e = os.environ["KK_LIST"]
            from collections import Counter
            cnt = Counter(op.fn.__code__.co_firstlineno for op in self.ops[e] if op.fn)
            print(f"[{e} ops by lambda lineno]", dict(sorted(cnt.items())))
        chunk = int(os.environ.get("BASS_BB_CHUNK", "400"))
        pos = {e: 0 for e in ENGS}
        done = False
        while not done:
            done = True
            for e in ENGS:
                n = len(self.ops[e])
                if pos[e] < n:
                    lo, hi = pos[e], min(pos[e] + chunk, n)
                    section[e](lambda eh, e=e, lo=lo, hi=hi: run_engine(e, eh, lo, hi))
                    pos[e] = hi
                    if pos[e] < n:
                        done = False

    def close(self):
        self._stack.close()

    def stats(self):
        return {e: (len(self.ops[e]), len(self.need_inc[e])) for e in ENGS} | {
            q: self.dmacnt[q] for q in QUEUES}



B, N, DM, H, DH = 16, 2048, 1024, 16, 64
NCORES = 8
SEQ_PER_CORE = B // NCORES
NT = N // 128
NGROUP = 8
DHID = 4096
EPS_LN = 1e-5
EPS_LA = 1e-6

bf16 = ml_dtypes.bfloat16



def build_program(debug_names=()):
    from contextlib import ExitStack
    nc = bass.Bass("TRN2", target_bir_lowering=False)
    S = Sched()
    st = ExitStack()
    dumps = {}

    def sb(name, shape, dt):
        return st.enter_context(nc.sbuf_tensor(name, shape, dt))

    def psum(name, shape, dt=F32):
        return st.enter_context(nc.psum_tensor(name, shape, dt))

    din = lambda name, shape, dt=F32: nc.dram_tensor(name, shape, dt, kind="ExternalInput").ap()
    dout = lambda name, shape, dt=F32: nc.dram_tensor(name, shape, dt, kind="ExternalOutput").ap()
    dint = lambda name, shape, dt: nc.dram_tensor(name, shape, dt, kind="Internal").ap()

    x_hbm = din("x", [SEQ_PER_CORE, N, DM])
    y_hbm = dout("y", [SEQ_PER_CORE, N, DM])
    w_qkl = din("w_qkl", [NGROUP, 128, 8, 256])
    w_kvv = din("w_kvv", [NGROUP, 128, 8, 384])
    w_g = din("w_g", [128, 8, 16, 128])
    w_go = din("w_go", [128, 8])
    w_o = din("w_o", [128, 8, 1024])
    w_haar = din("w_haar", [128, 8, 64])
    c_haarsc = din("c_haarsc", [128, 4])
    c_bg = din("c_bg", [128, 8])
    c_bgo = din("c_bgo", [128, 1])
    c_bo = din("c_bo", [128, 1024])
    c_s3 = din("c_s3", [128, 1024])
    w_1 = din("w_1", [32, 128, 8, 128])
    w_2 = din("w_2", [32, 128, 8, 128])
    w_h1 = din("w_h1", [32, 128, 32, 128])
    w_h2 = din("w_h2", [32, 128, 32, 128])
    w_3 = din("w_3", [128, 32, 1024])
    c_s1 = din("c_s1", [128, 32])
    c_s2 = din("c_s2", [128, 32])
    c_s1h = din("c_s1h", [128, 32])
    c_s2h = din("c_s2h", [128, 32])
    c_ident = din("c_ident", [128, 128], BF16)
    c_mlin = din("c_mlin", [128, 256], BF16)
    c_mloc0 = din("c_mloc0", [128, 256], BF16)
    c_mlocn = din("c_mlocn", [128, 256], BF16)
    c_lbt = din("c_lbt", [128, 4, 128], BF16)
    c_lenr = din("c_lenr", [128, 4])
    c_ones = din("c_ones", [128, 128], BF16)

    qklb = dint("qklb", [NGROUP, 128, 8, 256], BF16)
    kvvb = dint("kvvb", [NGROUP, 128, 8, 384], BF16)
    wgb_d = dint("wgb_d", [128, 8, 16, 128], BF16)
    wob_d = dint("wob_d", [128, 8, 1024], BF16)
    w1b = dint("w1b", [32, 128, 8, 128], BF16)
    w2b = dint("w2b", [32, 128, 8, 128], BF16)
    h1b = dint("h1b", [32, 128, 32, 128], BF16)
    h2b = dint("h2b", [32, 128, 32, 128], BF16)
    w3b = dint("w3b", [128, 32, 1024], BF16)
    y1d = dint("y1d", [SEQ_PER_CORE, NT, 128, 1024], F32)

    H_ar = sb("H_ar", [128, 32768], BF16)
    G_ar = sb("G_ar", [128, 16384], BF16)
    WB = sb("WB", [128, 8192], BF16)
    SCR = sb("SCR", [128, 4096], F32)
    qkl_bf = sb("qkl_bf", [128, 2048], BF16)
    kvv_bf = sb("kvv_bf", [128, 3072], BF16)
    qf = sb("qf", [128, 2048], BF16)
    klf = sb("klf", [128, 2048], BF16)
    ktm = sb("ktm", [128, 2048], BF16)
    fktm = sb("fktm", [128, 2048], BF16)
    fkf = sb("fkf", [128, 2048], BF16)
    vtm = sb("vtm", [128, 2048], BF16)
    vltm = sb("vltm", [128, 2048], BF16)
    vlvl = sb("vlvl", [128, 2048], BF16)
    bmvf = sb("bmvf", [128, 2048], BF16)
    atil = sb("atil", [128, 768], BF16)
    snap = sb("snap", [128, 975], BF16)
    ptil = sb("ptil", [128, 1024], BF16)
    fscr = sb("fscr", [128, 512], BF16)
    acc = sb("acc", [128, 2048], F32)
    tmp1 = sb("tmp1", [128, 1024], BF16)
    dtile = sb("dtile", [128, 1024], BF16)
    dfm = sb("dfm", [128, 1024], BF16)
    gatet = sb("gatet", [128, 1024], BF16)
    alph = sb("alph", [128, 32], F32)
    abuf = sb("abuf", [128, 512], BF16)
    bbuf = sb("bbuf", [128, 512], BF16)
    stat = sb("stat", [128, 48], F32)
    sw_sb = sb("sw_sb", [128, 16], F32)
    ident = sb("ident", [128, 128], BF16)
    mlin = sb("mlin", [128, 256], BF16)
    mloc0 = sb("mloc0", [128, 256], BF16)
    mlocn = sb("mlocn", [128, 256], BF16)
    lbt = sb("lbt", [128, 512], BF16)
    lenr = sb("lenr", [128, 4], F32)
    onesb = sb("onesb", [128, 128], BF16)
    haarw = sb("haarw", [128, 512], BF16)
    haarsc = sb("haarsc", [128, 4], F32)
    bg_sb = sb("bg_sb", [128, 8], F32)
    bgo_sb = sb("bgo_sb", [128, 1], F32)
    s1_sb = sb("s1_sb", [128, 32], F32)
    s2_sb = sb("s2_sb", [128, 32], F32)
    s1h_sb = sb("s1h_sb", [128, 32], F32)
    s2h_sb = sb("s2h_sb", [128, 32], F32)
    wgo_sb = sb("wgo_sb", [128, 8], BF16)
    cpool = sb("cpool", [128, 2], F32)
    cpool2 = sb("cpool2", [128, 1], F32)

    ps_state = psum("ps_state", [64, 130])
    ps_A = psum("ps_A", [128, 512])
    ps_tr = psum("ps_tr", [128, 1024], BF16)
    trB = ps_tr[:, :].bitcast(F32)
    ps_q = [psum(f"ps_q{i}", [128, 512]) for i in range(4)]
    ps_nrm = psum("ps_nrm", [128, 64])

    B = S.buf
    b_consts = B("consts")
    b_Hs = [B(f"H{t}") for t in range(NT)]
    b_G = [B(f"G{t}") for t in range(NT)]
    b_WB = B("WB")
    b_SCR = B("SCR")
    b_qkl = B("qkl_bf"); b_kvv = B("kvv_bf")
    b_qf = B("qf"); b_klf = B("klf"); b_ktm = B("ktm"); b_fktm = B("fktm")
    b_fkf = B("fkf"); b_vtm = B("vtm"); b_vltm = B("vltm"); b_vlvl = B("vlvl")
    b_bmvf = B("bmvf"); b_atil = B("atil"); b_snap = B("snap"); b_ptil = B("ptil")
    b_fscr = B("fscr"); b_acc = B("acc"); b_tmp1 = B("tmp1")
    b_atil4 = [B(f"atil{i}") for i in range(4)]
    b_psA2 = [B("psA0", excl=True), B("psA1", excl=True)]
    b_ptil2 = [B("pt0"), B("pt1")]
    b_gatet = B("gatet"); b_alph = B("alph")
    b_abuf = B("abuf"); b_bbuf = B("bbuf"); b_stat = B("stat"); b_sw = B("sw_sb")
    b_ps_state = B("ps_state", excl=True); b_ps_A = B("ps_A", excl=True)
    b_ps_tr = B("ps_tr", excl=True)
    b_trB2 = [B("trB0", excl=True), B("trB1", excl=True)]
    b_ps_nrm = B("ps_nrm", excl=True)
    b_y1d = B("y1d")
    b_wcache = {n: B(n) for n in
                ["qklb", "kvvb", "wgb_d", "wob_d", "w1b", "w2b", "h1b", "h2b", "w3b"]}

    def dump(name, ap, reads):
        if name not in debug_names or name in dumps:
            return
        shape = list(ap.shape)
        d = nc.dram_tensor(f"dbg_{name}", shape, ap.dtype, kind="ExternalOutput").ap()
        S.dma("qpool", lambda d=d, ap=ap: nc.gpsimd.dma_start(out=d, in_=ap),
              reads=reads, is_output=True)
        dumps[name] = shape

    def mm(out, lhsT, rhs, start, stop, reads, writes, skip=False):
        S.op("pe", lambda o=out, l=lhsT, r=rhs, a=start, b=stop, sk=skip:
             nc.tensor.matmul(out=o, lhsT=l, rhs=r, start=a, stop=b,
                              skip_group_check=sk),
             reads=reads, writes=writes)

    def tr(out, in_, reads, writes):
        S.op("pe", lambda o=out, i=in_: nc.tensor.transpose(out=o, in_=i, identity=ident[:, :]),
             reads=list(reads) + [b_consts], writes=writes)

    def dve(meth, reads, writes, **kw):
        S.op("dve", lambda meth=meth, kw=kw: getattr(nc.vector, meth)(**kw),
             reads=reads, writes=writes)

    def act(meth, reads, writes, **kw):
        S.op("act", lambda meth=meth, kw=kw: getattr(nc.scalar, meth)(**kw),
             reads=reads, writes=writes)

    def pool(meth, reads, writes, **kw):
        S.op("pool", lambda meth=meth, kw=kw: getattr(nc.gpsimd, meth)(**kw),
             reads=reads, writes=writes)

    pool("memset", [], [b_consts], ap=cpool[:, 0:1], constant=0.0)
    pool("memset", [], [b_consts], ap=cpool[:, 1:2], constant=float(EPS_LN))
    nc.const_aps.register(F32, 0.0, cpool[:, 0:1])
    nc.const_aps.register(F32, float(EPS_LN), cpool[:, 1:2])
    pool("memset", [], [b_consts], ap=cpool2[:, 0:1], constant=-1.0)
    nc.const_aps.register(F32, -1.0, cpool2[:, 0:1])

    for dst, src in [
        (ident[:, :], c_ident), (mlin[:, :], c_mlin), (mloc0[:, :], c_mloc0),
        (mlocn[:, :], c_mlocn), (lbt[:, :], c_lbt.rearrange("p a b -> p (a b)")),
        (lenr[:, :], c_lenr), (onesb[:, :], c_ones),
        (haarsc[:, :], c_haarsc), (bgo_sb[:, :], c_bgo),
        (s1_sb[:, :], c_s1), (s2_sb[:, :], c_s2),
        (s1h_sb[:, :], c_s1h), (s2h_sb[:, :], c_s2h),
    ]:
        S.dma("qsp", lambda d=dst, s=src: nc.sync.dma_start(out=d, in_=s),
              writes=[b_consts])
    S.dma("qpool", lambda: nc.gpsimd.dma_start(
        out=haarw[:, :], in_=w_haar.rearrange("p a b -> p (a b)")), writes=[b_consts])
    S.dma("qsp", lambda: nc.sync.dma_start(out=bg_sb[:, :], in_=c_bg), writes=[b_consts])
    S.dma("qsp", lambda: nc.sync.dma_start(out=SCR[:, 2048:2056], in_=w_go), writes=[b_SCR])
    dve("tensor_copy", [b_SCR], [b_consts], out=wgo_sb[:, :], in_=SCR[:, 2048:2056])
    act("activation", [b_consts], [b_sw], out=sw_sb[:, 8:12], in_=haarsc[:, :],
        func=AF.Exp, bias=0.0, scale=1.0, accum_out=sw_sb[:, 5:6])
    act("activation", [b_sw], [b_sw], out=sw_sb[:, 6:7], in_=sw_sb[:, 5:6],
        func=AF.Ln, bias=0.0, scale=1.0)
    act("activation", [b_sw], [b_sw], out=sw_sb[:, 7:8], in_=sw_sb[:, 6:7],
        func=AF.Exp, bias=0.0, scale=-1.0)
    S.op("dve", lambda: nc.vector.tensor_scalar(
        out=sw_sb[:, 0:4], in0=sw_sb[:, 8:12], scalar1=sw_sb[:, 7:8],
        scalar2=None, op0=ALU.mult), reads=[b_sw], writes=[b_sw])
    act("activation", [b_sw], [b_sw], out=sw_sb[:, 12:16], in_=sw_sb[:, 0:4],
        func=AF.Ln, bias=0.0, scale=1.0)
    pool("memset", [], [b_ptil2[0], b_ptil2[1]], ap=ptil[:, :], constant=0.0)

    pc_jobs = []

    def pc_add(src4, dst4, items, bname):
        for i in range(items):
            pc_jobs.append((src4[i, :, :, :].rearrange("p a b -> p (a b)"),
                            dst4[i, :, :, :].rearrange("p a b -> p (a b)"), bname))

    pc_add(w_qkl, qklb, NGROUP, "qklb")
    pc_add(w_kvv, kvvb, NGROUP, "kvvb")
    pc_jobs.append((w_g.rearrange("p a b c -> p (a b c)"),
                    wgb_d.rearrange("p a b c -> p (a b c)"), "wgb_d"))
    pc_jobs.append((w_o.rearrange("p a b -> p (a b)"),
                    wob_d.rearrange("p a b -> p (a b)"), "wob_d"))
    for hhalf in range(4):
        pc_jobs.append((w_3[:, hhalf * 8:(hhalf + 1) * 8, :].rearrange("p a b -> p (a b)"),
                        w3b[:, hhalf * 8:(hhalf + 1) * 8, :].rearrange("p a b -> p (a b)"),
                        "w3b"))
    pc_add(w_1, w1b, 32, "w1b")
    pc_add(w_2, w2b, 32, "w2b")
    pc_add(w_h1, h1b, 32, "h1b")
    pc_add(w_h2, h2b, 32, "h2b")

    pc_state = {"i": 0}

    def plan_precast(frac):
        n = len(pc_jobs)
        hi = min(n, pc_state["i"] + max(1, int(n * frac + 1)))
        for k in range(pc_state["i"], hi):
            (sa, da, bn) = pc_jobs[k]
            S.dma("qpool", lambda d=da, s=sa: nc.gpsimd.dma_start(out=d, in_=s),
                  reads=[], writes=[b_wcache[bn]])
        pc_state["i"] = hi

    def rebind(newbufs, oldbufs):
        merged = {}
        for ob in oldbufs:
            for sp_, i in ob.wtoks.items():
                merged[sp_] = max(merged.get(sp_, -1), i)
            for (sp_, i) in ob.readers:
                merged[sp_] = max(merged.get(sp_, -1), i)
        for nb in newbufs:
            nb.wtoks = dict(merged)
            nb.readers = []

    b_lstrip = [B(f"ls{i}") for i in range(16)]
    b_gstrip = [B(f"gs{i}") for i in range(16)]
    b_psq = [B(f"psq{i}", excl=True) for i in range(4)]
    b_xt = [B("xt0"), B("xt1")]
    b_trash = B("trash")
    b_ao = B("ao")
    b_lt = [B("lt0"), B("lt1")]
    b_wb = [B("wb")]
    b_h = [B(f"hf{i}") for i in range(32)]
    b_g1 = [B(f"g1f{i}") for i in range(32)]
    b_y1s = [B("y1s0"), B("y1s1")]
    b_fout = B("fout")
    b_alp = b_alph

    AX = mybir.AxisListType.X

    def ts_op(out, in0, s1, s2, op0, op1, reads, writes):
        if s2 is None:
            S.op("dve", lambda o=out, i=in0, a=s1, p0=op0:
                 nc.vector.tensor_scalar(out=o, in0=i, scalar1=a, scalar2=None, op0=p0),
                 reads=reads, writes=writes)
        else:
            S.op("dve", lambda o=out, i=in0, a=s1, b2=s2, p0=op0, p1=op1:
                 nc.vector.tensor_scalar(out=o, in0=i, scalar1=a, scalar2=b2, op0=p0, op1=p1),
                 reads=reads, writes=writes)

    def ln_chain(xt_ap, b_x_ap, out_ap, b_out, scol, extra_reads=()):
        trash = SCR[:, 3072:4096]
        act("activation", [b_x_ap] + list(extra_reads), [b_trash, b_stat],
            out=trash, in_=xt_ap, func=AF.Square, accum_out=stat[:, scol + 1:scol + 2])
        act("activation", [b_x_ap], [b_trash, b_stat],
            out=trash, in_=xt_ap, func=AF.Copy, accum_out=stat[:, scol + 0:scol + 1])
        act("activation", [b_stat], [b_stat], out=stat[:, scol + 2:scol + 3],
            in_=stat[:, scol:scol + 1], func=AF.Copy, bias=0.0, scale=1.0 / 1024)
        act("activation", [b_stat], [b_stat], out=stat[:, scol + 3:scol + 4],
            in_=stat[:, scol + 2:scol + 3], func=AF.Square, bias=0.0, scale=1.0)
        S.op("dve", lambda o=stat[:, scol + 4:scol + 5], i=stat[:, scol + 1:scol + 2],
             m2=stat[:, scol + 3:scol + 4]:
             nc.vector.scalar_tensor_tensor(out=o, in0=i, scalar=1.0 / 1024, in1=m2,
                                            op0=ALU.mult, op1=ALU.subtract),
             reads=[b_stat], writes=[b_stat])
        act("activation", [b_stat], [b_stat], out=stat[:, scol + 5:scol + 6],
            in_=stat[:, scol + 4:scol + 5], func=AF.Ln, bias=float(EPS_LN), scale=1.0)
        act("activation", [b_stat], [b_stat], out=stat[:, scol + 6:scol + 7],
            in_=stat[:, scol + 5:scol + 6], func=AF.Exp, bias=0.0, scale=-0.5)
        ts_op(out_ap, xt_ap, stat[:, scol + 2:scol + 3], stat[:, scol + 6:scol + 7],
              ALU.subtract, ALU.mult, [b_x_ap, b_stat], [b_out])

    def phi3(ps_or_sbuf_ap, b_in, out_ap, b_out, fs_off, width, fs_ap=None):
        fsl = fscr[:, fs_off:fs_off + width] if fs_ap is None else fs_ap
        ts_op(fsl, ps_or_sbuf_ap, 0.0, None, ALU.min, None, [b_in], [b_fscr])
        act("activation", [b_fscr], [b_fscr], out=fsl, in_=fsl, func=AF.Exp,
            bias=0.0, scale=1.0)
        ts_op(out_ap, ps_or_sbuf_ap, 0.0, None, ALU.max, None, [b_in], [b_out])
        dve("tensor_tensor", [b_out, b_fscr], [b_out], out=out_ap, in0=out_ap,
            in1=fsl, op=ALU.add)

    def G_ap(t):
        return G_ar[:, :].rearrange("p (a b) -> p a b", a=8)[:, :, t * 128:(t + 1) * 128]

    def strip_ap(glob, t):
        off = 16384 if glob else 0
        return H_ar[:, off:off + 16384].rearrange("p (hh q) -> p hh q", hh=16)[
            :, :, 64 * t:64 * t + 64]

    def bc16(ap_col16):
        a = ap_col16.rearrange("p (a b) -> p a b", b=1)
        return bass.AP(a.tensor, a.offset, [list(x) for x in a.ap[:-1]] + [[0, 64]])

    def phase_x(s):
        rebind(b_G, b_g1 + b_G)
        rebind(b_xt + [b_trash, b_ao], b_xt + [b_trash, b_ao, b_SCR] + b_y1s + [b_fout])
        if _os.environ.get("KK_NOX", "0") == "1":
            pool("memset", [], b_G, ap=G_ar[:, 0:1024], constant=0.01)
            return
        for t in range(NT):
            sl = t % 2
            xt = SCR[:, sl * 1024:(sl + 1) * 1024]
            S.dma("qsp", lambda d=xt, s_=s, t=t: nc.sync.dma_start(
                out=d, in_=x_hbm[s_, t * 128:(t + 1) * 128, :]), writes=[b_xt[sl]])
            lt = (dtile, dfm)[sl]
            ln_chain(xt, b_xt[sl], lt[:, :], b_lt[sl], (t % 2) * 8)
            for fc in range(8):
                tr(ps_tr[:, fc * 128:(fc + 1) * 128], lt[:, fc * 128:(fc + 1) * 128],
                   [b_lt[sl]], [b_ps_tr] if fc in (0, 7) else [])
            act("copy", [b_ps_tr], [b_G[t]],
                out=G_ap(t), in_=ps_tr[:, 0:1024].rearrange("p (a b) -> p a b", a=8))
            if s == 0 and t == 0:
                dump("xt0", SCR[:, 0:1024], [b_xt[0]])
                dump("stat0", stat[:, 0:8], [b_stat])
                dump("lt0", dtile[:, :], [b_lt[0]])
        dump("ln1_fm", G_ar[:, 0:4096], list(b_G))

    def load_group_weights(g):
        S.dma("qsp", lambda g=g: nc.sync.dma_start(
            out=qkl_bf[:, :], in_=qklb[g, :, :, :].rearrange("p a b -> p (a b)")),
            reads=[b_wcache["qklb"]], writes=[b_qkl])
        S.dma("qsp", lambda g=g: nc.sync.dma_start(
            out=kvv_bf[:, :], in_=kvvb[g, :, :, :].rearrange("p a b -> p (a b)")),
            reads=[b_wcache["kvvb"]], writes=[b_kvv])

    def fm_proj(g):
        for which, wcol, dst, bdst in ((0, 0, qf, b_qf), (1, 128, klf, b_klf)):
            for tq in range(4):
                ps = ps_q[tq]
                for m in range(8):
                    mm(ps[:, 0:512],
                       qkl_bf[:, m * 256 + wcol:m * 256 + wcol + 128],
                       G_ar[:, m * 2048 + tq * 512:m * 2048 + (tq + 1) * 512],
                       start=(m == 0), stop=(m == 7),
                       reads=[b_qkl] + [b_G[tq * 4 + i] for i in range(4)],
                       writes=[b_psq[tq]] if m in (0, 7) else [])
                act("copy", [b_psq[tq]], [bdst],
                    out=dst[:, tq * 512:(tq + 1) * 512], in_=ps[:, 0:512])

    def tm_proj(g):
        import os as _os2
        TMLVL = int(_os2.environ.get("KK_TM", "5"))
        for t in range(NT):
            sl = t % 4
            ps = ps_q[sl]
            slc = 0
            for m in range(8):
                mm(ps[:, slc:slc + 384],
                   G_ar[:, m * 2048 + t * 128:m * 2048 + (t + 1) * 128],
                   kvv_bf[:, m * 384:(m + 1) * 384],
                   start=(m == 0), stop=(m == 7),
                   reads=[b_kvv, b_G[t]],
                   writes=[b_psq[sl]] if m in (0, 7) else [])
            if TMLVL >= 2:
                act("copy", [b_psq[sl]], [b_ktm],
                    out=ktm[:, t * 128:(t + 1) * 128], in_=ps[:, slc:slc + 128])
            if TMLVL == 6:
                src_ap = (ps[:, slc:slc + 128]
                          if _os.environ.get("KK_TSRC", "ps") == "ps"
                          else ktm[:, t * 128:(t + 1) * 128])
                src_b = b_psq[sl] if _os.environ.get("KK_TSRC", "ps") == "ps" else b_ktm
                ts_op(fscr[:, (t % 4) * 128:(t % 4) * 128 + 128], src_ap,
                      0.0, None, ALU.min, None, [src_b], [b_fscr])
                dve("tensor_copy", [b_fscr], [b_fktm],
                    out=fktm[:, t * 128:(t + 1) * 128],
                    in_=fscr[:, (t % 4) * 128:(t % 4) * 128 + 128])
            elif TMLVL == 7:
                fsl = fscr[:, (t % 4) * 128:(t % 4) * 128 + 128]
                ts_op(fsl, ps[:, slc:slc + 128], 0.0, None, ALU.min, None,
                      [b_psq[sl]], [b_fscr])
                act("activation", [b_fscr], [b_fscr], out=fsl, in_=fsl, func=AF.Exp,
                    bias=0.0, scale=1.0)
                dve("tensor_copy", [b_fscr], [b_fktm],
                    out=fktm[:, t * 128:(t + 1) * 128], in_=fsl)
            elif TMLVL >= 3:
                phi3(ps[:, slc:slc + 128], b_psq[sl],
                     fktm[:, t * 128:(t + 1) * 128], b_fktm, (t % 4) * 128, 128)
            if TMLVL >= 4:
                ts_op(vtm[:, t * 128:(t + 1) * 128], ps[:, slc + 128:slc + 256],
                      0.0, None, ALU.add, None, [b_psq[sl]], [b_vtm])
            if TMLVL >= 5:
                act("copy", [b_psq[sl]], [b_vltm],
                    out=vltm[:, t * 128:(t + 1) * 128], in_=ps[:, slc + 256:slc + 384])

    def transpose_2048(srcap, b_src, dstap, b_dst):
        for t in range(NT):
            tr(ps_tr[:, (t % 8) * 128:(t % 8) * 128 + 128],
               srcap[:, t * 128:(t + 1) * 128],
               [b_src], [b_ps_tr] if t % 8 in (0, 7) else [])
            if t % 8 == 7:
                act("copy", [b_ps_tr], [b_dst],
                    out=dstap[:, (t - 7) * 128:(t + 1) * 128], in_=ps_tr[:, 0:1024])

    def local_attn(g):
        LOC = int(_os.environ.get("KK_LOC", "5"))

        def emit_pv(qb):
            sl = (qb % 2) * 512
            for h in (0, 1):
                pso = ps_q[2 * h + qb // 8]
                bq = b_psq[2 * h + qb // 8]
                col = (qb % 8) * 64
                bank_first = qb % 8 == 0
                bank_last = qb % 8 == 7
                mm(pso[:, col:col + 64],
                   ptil[:, sl + h * 256:sl + h * 256 + 128],
                   vltm[:, qb * 128 + 64 * h:qb * 128 + 64 * h + 64],
                   start=bank_first, stop=(bank_last and qb == 0),
                   reads=[b_ptil2[qb % 2], b_vltm],
                   writes=[bq] if bank_first else [])
                mm(ps_nrm[:, h * 16 + qb:h * 16 + qb + 1],
                   ptil[:, sl + h * 256:sl + h * 256 + 128], onesb[:, 0:1],
                   start=(qb == 0 and h == 0), stop=False,
                   reads=[b_ptil2[qb % 2], b_consts],
                   writes=[b_ps_nrm] if (qb == 0 and h == 0) else [])
                if qb > 0:
                    mm(pso[:, col:col + 64],
                       ptil[:, sl + h * 256 + 128:sl + h * 256 + 256],
                       vltm[:, (qb - 1) * 128 + 64 * h:(qb - 1) * 128 + 64 * h + 64],
                       start=False, stop=bank_last,
                       reads=[b_ptil2[qb % 2], b_vltm],
                       writes=[bq] if bank_last else [])
                    mm(ps_nrm[:, h * 16 + qb:h * 16 + qb + 1],
                       ptil[:, sl + h * 256 + 128:sl + h * 256 + 256], onesb[:, 0:1],
                       start=False, stop=(qb == 15 and h == 1),
                       reads=[b_ptil2[qb % 2], b_consts],
                       writes=[b_ps_nrm] if (qb == 15 and h == 1) else [])

        for qb in range(NT):
            mms = []
            for h in (0, 1):
                base = (ps_A, trB)[h]
                bb = (b_ps_A, b_trB2[0])[h]
                mms.append((h, bb, base[:, 0:128],
                            klf[64 * h:64 * h + 64, qb * 128:(qb + 1) * 128]))
                if qb > 0:
                    bb2 = (b_ps_A, b_trB2[1])[h]
                    mms.append((h, bb2, base[:, 128:256],
                                klf[64 * h:64 * h + 64, (qb - 1) * 128:qb * 128]))
            nmm = len(mms)
            for i, (h, bb, oap, lap) in enumerate(mms):
                first = i == 0 or (qb > 0 and i == 2) or (qb == 0 and i == 1)
                last = (qb > 0 and i in (1, 3)) or (qb == 0)
                mm(oap, lap, qf[64 * h:64 * h + 64, qb * 128:(qb + 1) * 128],
                   start=first, stop=last,
                   reads=[b_klf, b_qf] + ([b_ps_tr] if h == 1 and first else []),
                   writes=[bb] if (first or last) else [])
            sl = (qb % 2) * 512
            if LOC < 2:
                continue
            if qb == 0:
                act("activation", [b_ps_A], [b_ptil2[0]], out=ptil[:, sl:sl + 128],
                    in_=ps_A[:, 0:128], func=AF.Exp, bias=0.0, scale=0.125)
                act("activation", [b_trB2[0], b_ps_tr], [b_ptil2[0]],
                    out=ptil[:, sl + 256:sl + 384],
                    in_=trB[:, 0:128], func=AF.Exp, bias=0.0, scale=0.125)
            else:
                act("activation", [b_ps_A], [b_ptil2[qb % 2]], out=ptil[:, sl:sl + 256],
                    in_=ps_A[:, 0:256], func=AF.Exp, bias=0.0, scale=0.125)
                act("activation", [b_trB2[0], b_trB2[1], b_ps_tr], [b_ptil2[qb % 2]],
                    out=ptil[:, sl + 256:sl + 512],
                    in_=trB[:, 0:256], func=AF.Exp, bias=0.0, scale=0.125)
            if LOC < 3:
                continue
            dve("tensor_tensor", [b_ptil2[qb % 2], b_consts], [b_ptil2[qb % 2]],
                out=ptil[:, sl:sl + 512].rearrange("p (a b) -> p a b", a=2)[:, :, 0:128],
                in0=ptil[:, sl:sl + 512].rearrange("p (a b) -> p a b", a=2)[:, :, 0:128],
                in1=mloc0[:, :].rearrange("p (a b) -> p a b", a=2), op=ALU.mult)
            if qb > 0:
                dve("tensor_tensor", [b_ptil2[qb % 2], b_consts], [b_ptil2[qb % 2]],
                    out=ptil[:, sl:sl + 512].rearrange("p (a b) -> p a b", a=2)[:, :, 128:256],
                    in0=ptil[:, sl:sl + 512].rearrange("p (a b) -> p a b", a=2)[:, :, 128:256],
                    in1=mlocn[:, :].rearrange("p (a b) -> p a b", a=2), op=ALU.mult)
            if LOC >= 4 and qb >= 1:
                emit_pv(qb - 1)
        if LOC >= 4:
            emit_pv(15)
        if LOC < 5:
            return
        act("activation", [b_ps_nrm], [b_stat], out=stat[:, 16:48],
            in_=ps_nrm[:, 0:32], func=AF.Ln, bias=0.0, scale=1.0)
        act("activation", [b_stat], [b_stat], out=stat[:, 16:48],
            in_=stat[:, 16:48], func=AF.Exp, bias=0.0, scale=-1.0)
        for h in (0, 1):
            hh = 2 * g + h
            for qb in range(16):
                ts_op(H_ar[:, hh * 1024 + qb * 64:hh * 1024 + (qb + 1) * 64],
                      ps_q[2 * h + qb // 8][:, (qb % 8) * 64:(qb % 8) * 64 + 64],
                      stat[:, 16 + 16 * h + qb:17 + 16 * h + qb], None, ALU.mult,
                      None, [b_psq[2 * h + qb // 8], b_stat], [b_lstrip[hh]])

    def lin_pass(g, p):
        vv, b_vv = (vtm, b_vtm) if p == 0 else (vlvl, b_vlvl)

        def intra(cc):
            for h in (0, 1):
                pso = ps_q[2 * h + cc // 8]
                bq = b_psq[2 * h + cc // 8]
                col = (cc % 8) * 64
                mm(pso[:, col:col + 64],
                   atil[:, (cc % 3) * 256 + h * 128:(cc % 3) * 256 + (h + 1) * 128],
                   vv[:, cc * 128 + 64 * h:cc * 128 + 64 * h + 64],
                   start=(cc in (0, 8)), stop=False,
                   reads=[b_atil4[cc % 3], b_vv],
                   writes=[bq] if cc in (0, 8) else [])
                mm(ps_nrm[:, h * 16 + cc:h * 16 + cc + 1],
                   atil[:, (cc % 3) * 256 + h * 128:(cc % 3) * 256 + (h + 1) * 128],
                   onesb[:, 0:1],
                   start=(cc == 0 and h == 0), stop=False,
                   reads=[b_atil4[cc % 3], b_consts],
                   writes=[b_ps_nrm] if (cc == 0 and h == 0) else [])

        def inter_batch(half):
            lo = max(1, half * 8)
            for h in (0, 1):
                if h == 1:
                    S.pe_fence()
                for cc in range(lo, half * 8 + 8):
                    pso = ps_q[2 * h + half]
                    bq = b_psq[2 * h + half]
                    col = (cc % 8) * 64
                    sn = snap[64 * h:64 * h + 64, 65 * (cc - 1):65 * (cc - 1) + 65]
                    last = cc == half * 8 + 7
                    mm(pso[:, col:col + 64],
                       qf[64 * h:64 * h + 64, cc * 128:(cc + 1) * 128],
                       sn[:, 0:64],
                       start=False, stop=last,
                       reads=[b_qf, b_snap],
                       writes=[bq] if last else [])
                    mm(ps_nrm[:, h * 16 + cc:h * 16 + cc + 1],
                       qf[64 * h:64 * h + 64, cc * 128:(cc + 1) * 128],
                       sn[:, 64:65],
                       start=False, stop=(cc == 15 and h == 1),
                       reads=[b_qf, b_snap],
                       writes=[b_ps_nrm] if (cc == 15 and h == 1) else [])
                if h == 1:
                    S.pe_fence()

        for c in range(16):
            if c < 15:
                for h in (0, 1):
                    st = (c == 0 and h == 0)
                    mm(ps_state[0:64, 65 * h:65 * h + 64],
                       fktm[:, c * 128 + 64 * h:c * 128 + 64 * h + 64],
                       vv[:, c * 128 + 64 * h:c * 128 + 64 * h + 64],
                       start=st, stop=False, skip=(c > 0),
                       reads=[b_fktm, b_vv],
                       writes=[b_ps_state] if h == 0 else [])
                    mm(ps_state[0:64, 65 * h + 64:65 * h + 65],
                       fktm[:, c * 128 + 64 * h:c * 128 + 64 * h + 64],
                       onesb[:, 0:1],
                       start=False, stop=(h == 1), skip=(c > 0),
                       reads=[b_fktm, b_consts],
                       writes=[b_ps_state] if h == 1 else [])
                act("copy", [b_ps_state], [b_snap],
                    out=snap[0:64, 65 * c:65 * c + 65], in_=ps_state[0:64, 0:65])
                act("copy", [b_ps_state], [b_snap],
                    out=snap[64:128, 65 * c:65 * c + 65], in_=ps_state[0:64, 65:130])
            for h in (0, 1):
                base = (ps_A, trB)[h]
                bb = (b_psA2[c % 2], b_trB2[c % 2])[h]
                mm(base[:, (c % 2) * 128:(c % 2) * 128 + 128],
                   fkf[64 * h:64 * h + 64, c * 128:(c + 1) * 128],
                   qf[64 * h:64 * h + 64, c * 128:(c + 1) * 128],
                   start=True, stop=True,
                   reads=[b_fkf, b_qf] + ([b_ps_tr] if h == 1 else []),
                   writes=[bb])
            dve("tensor_tensor", [b_psA2[c % 2], b_consts], [b_atil4[c % 3]],
                out=atil[:, (c % 3) * 256:(c % 3) * 256 + 128],
                in0=ps_A[:, (c % 2) * 128:(c % 2) * 128 + 128],
                in1=mlin[:, 0:128], op=ALU.mult)
            dve("tensor_tensor", [b_trB2[c % 2], b_ps_tr, b_consts],
                [b_atil4[c % 3]],
                out=atil[:, (c % 3) * 256 + 128:(c % 3) * 256 + 256],
                in0=trB[:, (c % 2) * 128:(c % 2) * 128 + 128],
                in1=mlin[:, 0:128], op=ALU.mult)
            if c >= 1:
                intra(c - 1)
            if c == 8:
                inter_batch(0)
        intra(15)
        inter_batch(1)
        ts_op(stat[:, 16:48], ps_nrm[:, 0:32], float(EPS_LA), None, ALU.max, None,
              [b_ps_nrm], [b_stat])
        act("activation", [b_stat], [b_stat], out=stat[:, 16:48],
            in_=stat[:, 16:48], func=AF.Ln, bias=0.0, scale=1.0)
        if p == 0:
            act("activation", [b_stat], [b_stat], out=stat[:, 16:48],
                in_=stat[:, 16:48], func=AF.Exp, bias=0.0, scale=-1.0)
        else:
            act("activation", [b_stat, b_sw], [b_stat], out=stat[:, 16:48],
                in_=stat[:, 16:48], func=AF.Exp, bias=sw_sb[:, 11 + p:12 + p],
                scale=-1.0)
        for h in (0, 1):
            for cc in range(16):
                pslice = ps_q[2 * h + cc // 8][:, (cc % 8) * 64:(cc % 8) * 64 + 64]
                rsc = stat[:, 16 + 16 * h + cc:17 + 16 * h + cc]
                if p == 0:
                    ts_op(acc[:, h * 1024 + cc * 64:h * 1024 + (cc + 1) * 64],
                          pslice, rsc, None, ALU.mult, None,
                          [b_psq[2 * h + cc // 8], b_stat], [b_acc])
                else:
                    ts_op(tmp1[:, cc * 64:(cc + 1) * 64], pslice, rsc, None,
                          ALU.mult, None, [b_psq[2 * h + cc // 8], b_stat],
                          [b_tmp1])
            if p > 0:
                dve("tensor_add", [b_tmp1, b_acc], [b_acc],
                    out=acc[:, h * 1024:(h + 1) * 1024],
                    in0=acc[:, h * 1024:(h + 1) * 1024], in1=tmp1[:, :])

    def haar_prep(g, lvl):
        for which, (srcap, b_src, fmdst, b_fmdst) in enumerate(
                ((ktm, b_ktm, klf, b_klf), (vtm, b_vtm, bmvf, b_bmvf))):
            for t in range(NT):
                sl = t % 8
                pq = ps_q[sl // 4]
                bq = b_psq[sl // 4]
                mm(pq[:, (sl % 4) * 128:(sl % 4 + 1) * 128],
                   lbt[:, lvl * 128:(lvl + 1) * 128],
                   srcap[:, t * 128:(t + 1) * 128],
                   start=True, stop=True,
                   reads=[b_consts, b_src],
                   writes=[bq] if sl % 4 in (0, 3) else [])
                if sl % 4 == 3:
                    if which == 0:
                        ts_op(vltm[:, (t - 3) * 128:(t + 1) * 128],
                              pq[:, 0:512], lenr[:, lvl:lvl + 1], None,
                              ALU.mult, None, [bq, b_consts], [b_vltm])
                    else:
                        act("activation", [bq, b_consts], [b_vltm],
                            out=vltm[:, (t - 3) * 128:(t + 1) * 128],
                            in_=pq[:, 0:512], func=AF.Copy,
                            scale=lenr[:, lvl:lvl + 1], bias=0.0)
            transpose_2048(vltm, b_vltm, fmdst, b_fmdst)
        for which, (fmsrc, b_fm, wcol) in enumerate(
                ((klf, b_klf, lvl * 2), (bmvf, b_bmvf, lvl * 2 + 1))):
            for t in range(NT):
                sl = t % 4
                for h in (0, 1):
                    pq = ps_q[2 + h]
                    bq = b_psq[2 + h]
                    mm(pq[:, sl * 128:sl * 128 + 64],
                       fmsrc[64 * h:64 * h + 64, t * 128:(t + 1) * 128],
                       haarw[64 * h:64 * h + 64, wcol * 64:(wcol + 1) * 64],
                       start=True, stop=True,
                       reads=[b_fm, b_consts],
                       writes=[bq] if sl in (0, 3) else [])
                if sl == 3:
                    for h in (0, 1):
                        pq = ps_q[2 + h]
                        bq = b_psq[2 + h]
                        src_ap = pq[:, 0:512].rearrange("p (a b) -> p a b", a=4)[:, :, 0:64]
                        dst_ap = (fktm if which == 0 else vlvl)[
                            :, (t - 3) * 128:(t + 1) * 128].rearrange(
                            "p (a b) -> p a b", a=4)[:, :, 64 * h:64 * h + 64]
                        if which == 0:
                            fs_ap = fscr[:, h * 256:h * 256 + 256].rearrange(
                                "p (a b) -> p a b", a=4)
                            phi3(src_ap, bq, dst_ap, b_fktm, 0, 256, fs_ap=fs_ap)
                        else:
                            act("copy", [bq], [b_vlvl], out=dst_ap, in_=src_ap)
        transpose_2048(fktm, b_fktm, fkf, b_fkf)

    b_wg = B("wg"); b_wo_ = B("wo"); b_x2 = B("x2")
    b_w1s = [B("w1s0"), B("w1s1")]; b_w2s = [B("w2s0"), B("w2s1")]
    b_h1s = [B("h1s0"), B("h1s1")]; b_w3s = [B("w3s0"), B("w3s1")]

    def wg_tile(gc, fc):
        if gc < 4:
            return WB[:, (gc * 16 + fc) * 128:(gc * 16 + fc + 1) * 128]
        tens = (qf, klf, ktm, fktm)[gc - 4]
        return tens[:, fc * 128:(fc + 1) * 128]

    def gate_phase(s, g_dumps=False):
        rebind([b_wg], [b_wg, b_wo_] + b_h1s)
        rebind([b_qf, b_klf, b_ktm, b_fktm],
               [b_qf, b_klf, b_ktm, b_fktm])
        S.dma("qsp", lambda: nc.sync.dma_start(
            out=WB[:, 0:8192],
            in_=wgb_d[:, 0:4, :, :].rearrange("p a b c -> p (a b c)")),
            reads=[b_wcache["wgb_d"]], writes=[b_wg])
        for gi, tens in enumerate((qf, klf, ktm, fktm)):
            S.dma("qsp", lambda t_=tens, gi=gi: nc.sync.dma_start(
                out=t_[:, :],
                in_=wgb_d[:, 4 + gi, :, :].rearrange("p a b -> p (a b)")),
                reads=[b_wcache["wgb_d"]],
                writes=[(b_qf, b_klf, b_ktm, b_fktm)[gi]])
        pass
        for t in range(NT):
            sl = t % 2
            dve("tensor_tensor", [b_lstrip[i] for i in range(16)] +
                [b_gstrip[i] for i in range(16)], [b_lt[0]],
                out=dtile[:, :].rearrange("p (a b) -> p a b", a=16),
                in0=strip_ap(False, t), in1=strip_ap(True, t), op=ALU.subtract)
            for fc in range(8):
                tr(ps_tr[:, fc * 128:(fc + 1) * 128], dtile[:, fc * 128:(fc + 1) * 128],
                   [b_lt[0]], [b_ps_tr] if fc in (0, 7) else [])
            act("copy", [b_ps_tr], [b_lt[1]], out=dfm[:, :], in_=ps_tr[:, 0:1024])
            for gc in range(8):
                for fc in range(16):
                    rhs = (G_ar[:, fc * 2048 + t * 128:fc * 2048 + (t + 1) * 128]
                           if fc < 8 else dfm[:, (fc - 8) * 128:(fc - 7) * 128])
                    rb = [b_G[t]] if fc < 8 else [b_lt[1]]
                    wgb_src = ([b_wg] if gc < 4 else
                               [(b_qf, b_klf, b_ktm, b_fktm)[gc - 4]])
                    pq = ps_q[2 + gc // 4]
                    bq = b_psq[2 + gc // 4]
                    mm(pq[:, (gc % 4) * 128:(gc % 4 + 1) * 128],
                       wg_tile(gc, fc),
                       rhs, start=(fc == 0), stop=(fc == 15),
                       reads=wgb_src + rb,
                       writes=[bq] if (fc == 0 and gc % 4 == 0) or (fc == 15 and gc % 4 == 3) else [])
            for gc in range(8):
                act("activation", [b_psq[2 + gc // 4], b_consts], [b_gatet],
                    out=gatet[:, gc * 128:(gc + 1) * 128],
                    in_=ps_q[2 + gc // 4][:, (gc % 4) * 128:(gc % 4 + 1) * 128],
                    func=AF.Silu, bias=bg_sb[:, gc:gc + 1], scale=1.0)
            for gc in range(8):
                mm(ps_nrm[:, 0:1], gatet[:, gc * 128:(gc + 1) * 128],
                   wgo_sb[:, gc:gc + 1], start=(gc == 0), stop=(gc == 7),
                   reads=[b_gatet, b_consts],
                   writes=[b_ps_nrm] if gc in (0, 7) else [])
            act("activation", [b_ps_nrm, b_consts], [b_alp],
                out=alph[:, t:t + 1], in_=ps_nrm[:, 0:1], func=AF.Sigmoid,
                bias=bgo_sb[:, :], scale=1.0)
            act("activation", [b_alp, b_consts], [b_alp],
                out=alph[:, 16 + t:17 + t], in_=alph[:, t:t + 1], func=AF.Copy,
                bias=-1.0, scale=1.0)
            S.op("dve", lambda t=t: nc.vector.scalar_tensor_tensor(
                out=gatet[:, :].rearrange("p (a b) -> p a b", a=16),
                in0=dtile[:, :].rearrange("p (a b) -> p a b", a=16),
                scalar=alph[:, 16 + t:17 + t], in1=strip_ap(False, t),
                op0=ALU.mult, op1=ALU.add),
                reads=[b_lt[0], b_alp] + [b_lstrip[i] for i in range(16)],
                writes=[b_gatet])
            for fc in range(8):
                tr(ps_tr[:, fc * 128:(fc + 1) * 128], gatet[:, fc * 128:(fc + 1) * 128],
                   [b_gatet], [b_ps_tr] if fc in (0, 7) else [])
            act("copy", [b_ps_tr], [b_G[t]],
                out=G_ap(t), in_=ps_tr[:, 0:1024].rearrange("p (a b) -> p a b", a=8))
        if g_dumps:
            dump("alpha", alph[:, 0:16], [b_alp])
            dump("mixed_fm", G_ar[:, 0:2048], list(b_G))

    def wo_phase(s, g_dumps=False):
        rebind([b_wo_], [b_wg])
        S.dma("qsp", lambda: nc.sync.dma_start(
            out=WB[:, 0:8192], in_=wob_d.rearrange("p a b -> p (a b)")),
            reads=[b_wcache["wob_d"]], writes=[b_wo_])
        rebind([b_ptil2[0], b_ptil2[1]], b_ptil2)
        S.dma("qsp", lambda: nc.sync.dma_start(out=SCR[:, 3072:4096], in_=c_bo),
              writes=[b_trash])
        dve("tensor_copy", [b_trash], [b_ptil2[0]], out=ptil[:, 0:1024],
            in_=SCR[:, 3072:4096])
        rebind(b_xt + [b_ao, b_trash], b_xt + [b_ao, b_trash])
        for t in range(NT):
            sl = t % 2
            for m in range(8):
                for hf in (0, 1):
                    mm(ps_q[hf][:, 0:512],
                       G_ar[:, m * 2048 + t * 128:m * 2048 + (t + 1) * 128],
                       WB[:, m * 1024 + hf * 512:m * 1024 + (hf + 1) * 512],
                       start=(m == 0), stop=(m == 7),
                       reads=[b_G[t], b_wo_],
                       writes=[b_psq[hf]] if (m in (0, 7)) else [])
            xt = SCR[:, sl * 1024:(sl + 1) * 1024]
            S.dma("qsp", lambda d=xt, s_=s, t=t: nc.sync.dma_start(
                out=d, in_=x_hbm[s_, t * 128:(t + 1) * 128, :]), writes=[b_xt[sl]])
            ao = SCR[:, 2048:3072]
            for hf in (0, 1):
                dve("tensor_tensor", [b_psq[hf], b_ptil2[0]], [b_ao],
                    out=SCR[:, 2048 + hf * 512:2048 + (hf + 1) * 512],
                    in0=ps_q[hf][:, 0:512],
                    in1=ptil[:, hf * 512:(hf + 1) * 512], op=ALU.add)
            ln_chain(ao, b_ao, ao, b_ao, (t % 2) * 8)
            dve("tensor_tensor", [b_ao, b_xt[sl]], [b_xt[sl]],
                out=xt, in0=ao, in1=xt, op=ALU.add)
            S.dma("qpool", lambda d=t, s_=s, src=xt: nc.gpsimd.dma_start(
                out=y1d[s_, d, :, :], in_=src), reads=[b_xt[sl]], writes=[b_y1d])
            if g_dumps and t == 0:
                dump("y1_t0", xt, [b_xt[sl]])

    def ffn_phase(s, g_dumps=False):
        rebind(b_h, b_h + [b_lstrip[i] for i in range(16)] +
               [b_gstrip[i] for i in range(16)])
        rebind(b_g1, b_g1 + b_G)
        rebind(b_h1s, [b_wo_] + b_h1s)
        rebind([b_vtm, b_vlvl], [b_vtm, b_vlvl])
        rebind(b_w1s, [b_qkl] + b_w1s)
        rebind(b_w2s, [b_kvv] + b_w2s)
        rebind(b_w3s, [b_klf] + b_w3s)
        rebind(b_y1s + [b_fout], b_xt + [b_ao, b_trash] + b_y1s + [b_fout])
        S.dma("qsp", lambda: nc.sync.dma_start(out=SCR[:, 3072:4096], in_=c_s3),
              writes=[b_trash])
        dve("tensor_copy", [b_trash], [b_ptil2[0]], out=ptil[:, 0:1024],
            in_=SCR[:, 3072:4096])

        def x2_ap(m):
            t_ = (vtm, vlvl)[m // 4]
            mm_ = m % 4
            return t_[:, mm_ * 512:(mm_ + 1) * 512]
        b_x2l = [b_vtm, b_vlvl]
        for tg in range(4):
            pass
            for tt in range(4):
                t = tg * 4 + tt
                ysl = tt % 2
                ys = SCR[:, ysl * 1024:(ysl + 1) * 1024]
                S.dma("qact", lambda d=ys, s_=s, t=t: nc.scalar.dma_start(
                    out=d, in_=y1d[s_, t, :, :]), reads=[b_y1d], writes=[b_y1s[ysl]])
                lt = (dtile, dfm)[ysl]
                ln_chain(ys, b_y1s[ysl], lt[:, :], b_lt[ysl], (tt % 2) * 8)
                for fc in range(8):
                    tr(ps_tr[:, fc * 128:(fc + 1) * 128], lt[:, fc * 128:(fc + 1) * 128],
                       [b_lt[ysl]], [b_ps_tr] if fc in (0, 7) else [])
                S.op("act", lambda tt=tt: nc.scalar.copy(
                    out=vtm[:, :].rearrange("p (a b) -> p a b", a=4)[
                        :, :, tt * 128:(tt + 1) * 128],
                    in_=ps_tr[:, 0:512].rearrange("p (a b) -> p a b", a=4)),
                    reads=[b_ps_tr], writes=[b_vtm])
                S.op("act", lambda tt=tt: nc.scalar.copy(
                    out=vlvl[:, :].rearrange("p (a b) -> p a b", a=4)[
                        :, :, tt * 128:(tt + 1) * 128],
                    in_=ps_tr[:, 512:1024].rearrange("p (a b) -> p a b", a=4)),
                    reads=[b_ps_tr], writes=[b_vlvl])
            for oc in range(32):
                sl = oc % 2
                S.dma("qsp", lambda oc=oc, sl=sl: nc.sync.dma_start(
                    out=qkl_bf[:, sl * 1024:(sl + 1) * 1024],
                    in_=w1b[oc, :, :, :].rearrange("p a b -> p (a b)")),
                    reads=[b_wcache["w1b"]], writes=[b_w1s[sl]])
                S.dma("qsp", lambda oc=oc, sl=sl: nc.sync.dma_start(
                    out=kvv_bf[:, sl * 1024:(sl + 1) * 1024],
                    in_=w2b[oc, :, :, :].rearrange("p a b -> p (a b)")),
                    reads=[b_wcache["w2b"]], writes=[b_w2s[sl]])
                bps = (b_psq[sl * 2], b_psq[sl * 2 + 1])
                for m in range(8):
                    mm(ps_q[sl * 2][:, 0:512],
                       qkl_bf[:, sl * 1024 + m * 128:sl * 1024 + (m + 1) * 128],
                       x2_ap(m),
                       start=(m == 0), stop=(m == 7),
                       reads=[b_w1s[sl], b_x2l[m // 4]],
                       writes=[bps[0]] if m in (0, 7) else [])
                for m in range(8):
                    mm(ps_q[sl * 2 + 1][:, 0:512],
                       kvv_bf[:, sl * 1024 + m * 128:sl * 1024 + (m + 1) * 128],
                       x2_ap(m),
                       start=(m == 0), stop=(m == 7),
                       reads=[b_w2s[sl], b_x2l[m // 4]],
                       writes=[bps[1]] if m in (0, 7) else [])
                act("activation", [bps[0], b_consts], [b_abuf], out=abuf[:, :],
                    in_=ps_q[sl * 2][:, 0:512], func=AF.Silu,
                    scale=s1_sb[:, oc:oc + 1], bias=0.0)
                ts_op(bbuf[:, :], ps_q[sl * 2 + 1][:, 0:512], s2_sb[:, oc:oc + 1], None,
                      ALU.mult, None, [bps[1], b_consts], [b_bbuf])
                dve("tensor_tensor", [b_abuf, b_bbuf], [b_h[oc]],
                    out=H_ar[:, oc * 512:(oc + 1) * 512], in0=abuf[:, :],
                    in1=bbuf[:, :], op=ALU.mult)
            for oc in range(32):
                sl = oc % 2
                S.dma("qsp", lambda oc=oc, sl=sl: nc.sync.dma_start(
                    out=WB[:, sl * 4096:(sl + 1) * 4096],
                    in_=h1b[oc, :, :, :].rearrange("p a b -> p (a b)")),
                    reads=[b_wcache["h1b"]], writes=[b_h1s[sl]])
                qi = oc % 4
                bps = b_psq[qi]
                for m in range(32):
                    mm(ps_q[qi][:, 0:512],
                       WB[:, sl * 4096 + m * 128:sl * 4096 + (m + 1) * 128],
                       H_ar[:, m * 512:(m + 1) * 512],
                       start=(m == 0), stop=(m == 31),
                       reads=[b_h1s[sl]] + ([b_h[i] for i in range(32)] if m == 0 else []),
                       writes=[bps] if m in (0, 31) else [])
                act("activation", [bps, b_consts], [b_g1[oc]],
                    out=G_ar[:, oc * 512:(oc + 1) * 512],
                    in_=ps_q[qi][:, 0:512], func=AF.Gelu,
                    scale=s1h_sb[:, oc:oc + 1], bias=0.0)
            for oc in range(32):
                sl = oc % 2
                S.dma("qsp", lambda oc=oc, sl=sl: nc.sync.dma_start(
                    out=WB[:, sl * 4096:(sl + 1) * 4096],
                    in_=h2b[oc, :, :, :].rearrange("p a b -> p (a b)")),
                    reads=[b_wcache["h2b"]], writes=[b_h1s[sl]])
                qi = oc % 4
                bps = b_psq[qi]
                for m in range(32):
                    mm(ps_q[qi][:, 0:512],
                       WB[:, sl * 4096 + m * 128:sl * 4096 + (m + 1) * 128],
                       G_ar[:, m * 512:(m + 1) * 512],
                       start=(m == 0), stop=(m == 31),
                       reads=[b_h1s[sl]] + ([b_g1[i] for i in range(32)] if m == 0 else []),
                       writes=[bps] if m in (0, 31) else [])
                act("activation", [bps, b_consts], [b_h[oc]],
                    out=H_ar[:, oc * 512:(oc + 1) * 512],
                    in_=ps_q[qi][:, 0:512], func=AF.Gelu,
                    scale=s2h_sb[:, oc:oc + 1], bias=0.0)
            pass
            for half in range(2):
                for m in range(32):
                    sl = m % 2
                    S.dma("qsp", lambda m=m, sl=sl: nc.sync.dma_start(
                        out=klf[:, sl * 1024:(sl + 1) * 1024], in_=w3b[:, m, :]),
                        reads=[b_wcache["w3b"]], writes=[b_w3s[sl]])
                    for tt2 in (0, 1):
                        tloc = half * 2 + tt2
                        for hf in (0, 1):
                            qi = tt2 * 2 + hf
                            mm(ps_q[qi][:, 0:512],
                               H_ar[:, m * 512 + tloc * 128:m * 512 + (tloc + 1) * 128],
                               klf[:, sl * 1024 + hf * 512:sl * 1024 + (hf + 1) * 512],
                               start=(m == 0), stop=(m == 31),
                               reads=[b_w3s[sl], b_h[m]],
                               writes=[b_psq[qi]] if m in (0, 31) else [])
                for tt2 in (0, 1):
                    t = tg * 4 + half * 2 + tt2
                    fo = SCR[:, 2048:3072]
                    for hf in (0, 1):
                        dve("tensor_tensor", [b_psq[tt2 * 2 + hf], b_ptil2[0]], [b_fout],
                            out=SCR[:, 2048 + hf * 512:2048 + (hf + 1) * 512],
                            in0=ps_q[tt2 * 2 + hf][:, 0:512],
                            in1=ptil[:, hf * 512:(hf + 1) * 512], op=ALU.mult)
                    ln_chain(fo, b_fout, fo, b_fout, (tt2 % 2) * 8)
                    ysl = tt2
                    ys = SCR[:, ysl * 1024:(ysl + 1) * 1024]
                    S.dma("qact", lambda d=ys, s_=s, t=t: nc.scalar.dma_start(
                        out=d, in_=y1d[s_, t, :, :]), reads=[b_y1d], writes=[b_y1s[ysl]])
                    dve("tensor_tensor", [b_fout, b_y1s[ysl]], [b_y1s[ysl]],
                        out=ys, in0=fo, in1=ys, op=ALU.add)
                    S.dma("qpool", lambda src=ys, s_=s, t=t: nc.gpsimd.dma_start(
                        out=y_hbm[s_, t * 128:(t + 1) * 128, :], in_=src),
                        reads=[b_y1s[ysl]], is_output=True)
                if g_dumps and half == 0 and tg == 0:
                    dump("ffn_h", H_ar[:, 0:1024], [b_h[0], b_h[1]])

    import os as _os
    KK_GROUPS = int(_os.environ.get("KK_GROUPS", "8"))
    KK_LEVELS = int(_os.environ.get("KK_LEVELS", "4"))
    KK_SEQS = int(_os.environ.get("KK_SEQS", "2"))
    KK_PH = _os.environ.get("KK_PHASES", "xgtwf")
    KK_GS = _os.environ.get("KK_GSTAGES", "plqmh")

    def plan_seq(s):
        phase_x(s)
        if "g" not in KK_PH:
            return
        if s == 0:
            plan_precast(2 / 14)
        load_group_weights(0)
        for g in range(KK_GROUPS):
            dbg = (s == 0 and g == 0)
            pass
            if "p" not in KK_GS and "2" not in KK_GS and "3" not in KK_GS:
                continue
            if "p" in KK_GS or "2" in KK_GS:
                fm_proj(g)
            if "p" in KK_GS or "3" in KK_GS:
                tm_proj(g)
            if dbg:
                dump("qf_g0", qf[:, 0:2048], [b_qf])
                dump("klf_g0", klf[:, 0:2048], [b_klf])
                dump("ktm_g0", ktm[:, 0:2048], [b_ktm])
                dump("vtm_g0", vtm[:, 0:2048], [b_vtm])
                dump("vltm_g0", vltm[:, 0:2048], [b_vltm])
                dump("fktm_g0", fktm[:, 0:2048], [b_fktm])
            if g + 1 < KK_GROUPS:
                load_group_weights(g + 1)
            if "p" in KK_GS or "3" in KK_GS:
                transpose_2048(fktm, b_fktm, fkf, b_fkf)
            if dbg:
                dump("fkf_g0", fkf[:, 0:2048], [b_fkf])
            pass
            rebind([b_ps_A], [b_ps_A] + b_psA2)
            if "l" in KK_GS:
                local_attn(g)
                if dbg:
                    dump("ls_g0", H_ar[:, 0:2048], [b_lstrip[0], b_lstrip[1]])
            if "q" in KK_GS:
                for tq in range(4):
                    phi3(qf[:, tq * 512:(tq + 1) * 512], b_qf,
                         qf[:, tq * 512:(tq + 1) * 512], b_qf, 0, 512)
                if dbg:
                    dump("phiq_g0", qf[:, 0:2048], [b_qf])
            rebind(b_psA2, [b_ps_A] + b_psA2)
            if "m" in KK_GS:
                lin_pass(g, 0)
                if dbg:
                    dump("acc_main_g0", acc[:, :], [b_acc])
            for lvl in range(KK_LEVELS if "h" in KK_GS else 0):
                haar_prep(g, lvl)
                if dbg and lvl == 0:
                    dump("klvl0_fktm", fktm[:, 0:2048], [b_fktm])
                    dump("vlvl0", vlvl[:, 0:2048], [b_vlvl])
                lin_pass(g, lvl + 1)
            if "m" in KK_GS:
                dve("tensor_copy", [b_acc], [b_gstrip[2 * g], b_gstrip[2 * g + 1]],
                    out=H_ar[:, 16384 + 2 * g * 1024:16384 + (2 * g + 2) * 1024],
                    in_=acc[:, :])
                if dbg:
                    dump("gs_g0", H_ar[:, 16384:18432], [b_gstrip[0], b_gstrip[1]])
        if "t" in KK_PH:
            gate_phase(s, g_dumps=(s == 0))
        if "w" in KK_PH:
            wo_phase(s, g_dumps=(s == 0))
        if "f" in KK_PH:
            ffn_phase(s, g_dumps=(s == 0))

    if _os.environ.get("KK_NOPC", "0") != "1":
        plan_precast(1.0)
    plan_seq(0)
    if KK_SEQS > 1:
        plan_seq(1)
    S.barrier_wait("sp", S.out_dma_toks)

    with nc.Block() as block:
        S.finalize(nc, block, None)
    stats = S.stats()
    S.close()
    st.close()
    return nc, dumps, stats



def _host_inputs(inp):
    f32 = np.float32
    g = {}
    Wq = inp["L0_attn_Wq"]; Wkl = inp["L0_attn_Wkl"]
    Wk = inp["L0_attn_Wk"]; Wv = inp["L0_attn_Wv"]; Wvl = inp["L0_attn_Wvl"]
    qkl = np.empty((8, 128, 8, 256), f32)
    qkl[:, :, :, 0:128] = Wq.reshape(8, 128, 8, 128).transpose(0, 3, 2, 1)
    qkl[:, :, :, 128:256] = Wkl.reshape(8, 128, 8, 128).transpose(0, 3, 2, 1)
    g["w_qkl"] = qkl
    kvv = np.empty((8, 128, 8, 384), f32)
    kvv[:, :, :, 0:128] = Wk.reshape(8, 128, 8, 128).transpose(0, 3, 2, 1)
    kvv[:, :, :, 128:256] = Wv.reshape(8, 128, 8, 128).transpose(0, 3, 2, 1)
    kvv[:, :, :, 256:384] = Wvl.reshape(8, 128, 8, 128).transpose(0, 3, 2, 1)
    g["w_kvv"] = kvv
    Wg = inp["L0_attn_Wg"]
    g["w_g"] = np.ascontiguousarray(
        Wg.reshape(8, 128, 16, 128).transpose(3, 0, 2, 1))
    g["w_go"] = np.ascontiguousarray(inp["L0_attn_Wgo"].reshape(8, 128).T)
    Wo = inp["L0_attn_Wo"]
    g["w_o"] = np.ascontiguousarray(Wo.T.reshape(8, 128, 1024).transpose(1, 0, 2))
    hw = np.empty((128, 8, 64), f32)
    for lvl in range(4):
        kT = inp["L0_attn_haar_Wk"][lvl].T
        vT = inp["L0_attn_haar_Wv"][lvl].T
        hw[0:64, lvl * 2, :] = kT; hw[64:128, lvl * 2, :] = kT
        hw[0:64, lvl * 2 + 1, :] = vT; hw[64:128, lvl * 2 + 1, :] = vT
    g["w_haar"] = hw
    g["c_haarsc"] = np.tile(inp["L0_attn_haar_scale"].reshape(1, 4), (128, 1)).astype(f32)
    g["c_bg"] = np.ascontiguousarray(inp["L0_attn_bg"].reshape(8, 128).T.astype(f32))
    g["c_bgo"] = np.tile(inp["L0_attn_bgo"].reshape(1, 1), (128, 1)).astype(f32)
    g["c_bo"] = np.tile(inp["L0_attn_bo"].reshape(1, 1024), (128, 1)).astype(f32)
    g["c_s3"] = np.tile(inp["L0_ffn_w3_s"].reshape(1, 1024), (128, 1)).astype(f32)
    g["w_1"] = np.ascontiguousarray(
        inp["L0_ffn_w1_w"].reshape(32, 128, 8, 128).transpose(0, 3, 2, 1))
    g["w_2"] = np.ascontiguousarray(
        inp["L0_ffn_w2_w"].reshape(32, 128, 8, 128).transpose(0, 3, 2, 1))
    g["w_h1"] = np.ascontiguousarray(
        inp["L0_ffn_h1_w"].reshape(32, 128, 32, 128).transpose(0, 3, 2, 1))
    g["w_h2"] = np.ascontiguousarray(
        inp["L0_ffn_h2_w"].reshape(32, 128, 32, 128).transpose(0, 3, 2, 1))
    g["w_3"] = np.ascontiguousarray(
        inp["L0_ffn_w3_w"].T.reshape(32, 128, 1024).transpose(1, 0, 2))
    for nm, key in (("c_s1", "L0_ffn_w1_s"), ("c_s2", "L0_ffn_w2_s"),
                    ("c_s1h", "L0_ffn_h1_s"), ("c_s2h", "L0_ffn_h2_s")):
        g[nm] = np.ascontiguousarray(inp[key].reshape(32, 128).T.astype(f32))
    g["c_ident"] = np.eye(128, dtype=f32).astype(bf16)
    jj, ii = np.meshgrid(np.arange(128), np.arange(128), indexing="ij")
    mlin1 = (ii >= jj).astype(f32)
    g["c_mlin"] = np.tile(mlin1, (1, 2)).astype(bf16)
    mfull = ((ii >= jj) & (ii < jj + 64)).astype(f32)
    mprev = (jj > ii + 64).astype(f32)
    g["c_mloc0"] = np.tile(mfull, (1, 2)).astype(bf16)
    g["c_mlocn"] = np.tile(mprev, (1, 2)).astype(bf16)
    lbt = np.zeros((128, 4, 128), f32)
    lenr = np.zeros((128, 4), f32)
    p_ = np.arange(128)
    for lvl in range(4):
        b = 2 ** (lvl + 1)
        lbt[:, lvl, :] = ((p_[:, None] // b == p_[None, :] // b) &
                          (p_[:, None] <= p_[None, :])).astype(f32)
        lenr[:, lvl] = 1.0 / (p_ % b + 1)
    g["c_lbt"] = lbt.astype(bf16)
    g["c_lenr"] = lenr
    g["c_ones"] = np.ones((128, 128), f32).astype(bf16)
    return g


_CACHE = {}


def kernel(**inputs):
    inputs = {k: np.asarray(v) for k, v in inputs.items()}
    debug_names = _CACHE.get("debug_names", ())
    key = ("prog", tuple(sorted(debug_names)))
    if key not in _CACHE:
        _CACHE[key] = build_program(debug_names)
    nc, dumps, stats = _CACHE[key]
    shared = _host_inputs(inputs)
    x = np.ascontiguousarray(inputs["L0_attn_x"], dtype=np.float32)
    in_maps = []
    for c in range(NCORES):
        m = dict(shared)
        m["x"] = x[c * SEQ_PER_CORE:(c + 1) * SEQ_PER_CORE]
        in_maps.append(m)
    res = run_bass_kernel_spmd(nc, in_maps, core_ids=list(range(NCORES)))
    y = np.concatenate([np.asarray(r["y"], dtype=np.float32)
                        for r in res.results], axis=0)
    _CACHE["last_results"] = res
    _CACHE["last_dumps"] = dumps
    return y
